# version 24
# baseline (speedup 1.0000x reference)
import math
import numpy as np
import ml_dtypes
import concourse.bass as bass
import concourse.mybir as mybir
from concourse.bass_utils import run_bass_kernel_spmd
from contextlib import ExitStack

F32 = mybir.dt.float32
BF16 = mybir.dt.bfloat16
AF = mybir.ActivationFunctionType
ALU = mybir.AluOpType
BF = ml_dtypes.bfloat16

E = 2048
S = 2048
H = 16
D = 128
NT = 16
CH = 4
CW = 512
W2 = 256
NEC = E // W2
INV_SQRT_D = 1.0 / math.sqrt(128.0)
LN_EPS = 1e-5
FFN_EPS = 1e-10
DEBUG_DUMP = False
TRUNC = ""
DBG_NO_QUANT = False
DBG_V_TRASH = False
DBG_NO_WSL = False

SEM_NAMES = ["c", "bias", "x0", "x1", "x2", "x3", "wq", "wk", "wv", "txn", "txb", "tv0", "tv1",
             "tw0", "tw1", "so0", "so1", "sw0", "sw1", "oth0", "o0b", "o0c", "o0d",
             "oth1", "wc0", "wc1",
             "wo0", "wo1", "xs0", "xs1", "sy0", "sy1", "dv", "ac", "pe"]
OTH0 = ["oth0", "o0b", "o0c", "o0d"]


def build_nc():
    nc = bass.Bass("TRN2", target_bir_lowering=False, debug=False)
    x_d = nc.dram_tensor("x", [S, E], F32, kind="ExternalInput").ap()
    wq_d = nc.dram_tensor("wq", [E, E], F32, kind="ExternalInput").ap()
    wk_d = nc.dram_tensor("wk", [E, E], F32, kind="ExternalInput").ap()
    wv_d = nc.dram_tensor("wv", [E, E], F32, kind="ExternalInput").ap()
    wo_d = nc.dram_tensor("wo", [E, E], F32, kind="ExternalInput").ap()
    wf_d = nc.dram_tensor("wf", [E, E], F32, kind="ExternalInput").ap()
    bias_d = nc.dram_tensor("fbias", [1, E], F32, kind="ExternalInput").ap()
    c2_d = nc.dram_tensor("c2t", [128, S], F32, kind="ExternalInput").ap()
    s2_d = nc.dram_tensor("s2t", [128, S], F32, kind="ExternalInput").ap()
    tri_d = nc.dram_tensor("tri", [128, 128], BF16, kind="ExternalInput").ap()
    onc_d = nc.dram_tensor("onc", [128, 1], BF16, kind="ExternalInput").ap()
    oncf_d = nc.dram_tensor("oncf", [128, 1], F32, kind="ExternalInput").ap()
    onr_d = nc.dram_tensor("onr", [1, 128], BF16, kind="ExternalInput").ap()
    onm_d = nc.dram_tensor("onm", [128, 128], BF16, kind="ExternalInput").ap()
    onrf_d = nc.dram_tensor("onrf", [1, 128], F32, kind="ExternalInput").ap()
    y_d = nc.dram_tensor("y", [S, E], F32, kind="ExternalOutput").ap()
    scr_kind = "ExternalOutput" if DEBUG_DUMP else "Internal"
    outT_d = nc.dram_tensor("outT_scr", [H, 128, S], BF16, kind=scr_kind).ap()
    wtT_d = nc.dram_tensor("wtT_scr", [NT, 128, E], BF16, kind=scr_kind).ap()
    xnT_d = (nc.dram_tensor("xnT_dump", [128, NT * S], BF16, kind="ExternalOutput").ap()
             if DEBUG_DUMP else None)

    ctx = ExitStack()
    sb = lambda n_, s_, d_: ctx.enter_context(nc.sbuf_tensor(n_, s_, d_))
    ps = lambda n_, s_, d_: ctx.enter_context(nc.psum_tensor(n_, s_, d_))
    with ctx:
        xnT = sb("xnT", [128, NT * S], BF16)
        c2 = sb("c2", [128, S], F32)
        s2 = sb("s2", [128, S], F32)
        tri = sb("trim", [128, 128], BF16)
        onc = sb("oncs", [128, 1], BF16)
        oncf = sb("oncfs", [128, 1], F32)
        onr = sb("onrs", [1, 128], BF16)
        onm = sb("onms", [128, 128], BF16)
        onrf = sb("onrfs", [1, 128], F32)
        biasb = sb("biasb", [1, E], BF16)
        st6 = sb("st6", [128, 24], F32)
        mv = [sb(f"mv{i}", [128, 2], F32) for i in range(3)]
        rstd = [sb(f"rstd{i}", [128, 1], F32) for i in range(3)]
        epsv = sb("epsv", [128, 1], F32)
        bias3 = sb("bias3", [128, 1], F32)
        accw = sb("accw", [128, 16], F32)
        accr = sb("accr", [128, 1], F32)
        sclv = sb("sclv", [128, 1], F32)
        thrv = sb("thrv", [128, 1], F32)
        nthrv = sb("nthrv", [128, 1], F32)
        oTs = [sb(f"oTs{i}", [128, S], BF16) for i in range(2)]
        biasbc = sb("biasbc", [128, E], BF16)
        ctx1 = ExitStack()
        sb1 = lambda n_, s_, d_: ctx1.enter_context(nc.sbuf_tensor(n_, s_, d_))
        xst = [sb1(f"xst{i}", [128, E], F32) for i in range(3)]
        xnn = [sb1(f"xnn{i}", [128, E], BF16) for i in range(2)]
        wtbf = [sb1("wtbf0", [128, E], BF16)] * 4
        wtTs = [sb1("wtTs0", [128, NT * 128], BF16)] * 4
        wslq = [sb1(f"wslq{i}", [128, NT * 128], BF16) for i in range(2)]
        wslk = [sb1(f"wslk{i}", [128, NT * 128], BF16) for i in range(2)]
        wslv = [sb1(f"wslv{i}", [128, NT * 128], BF16) for i in range(2)]
        qTh = [sb1(f"qTh{i}", [128, S], BF16) for i in range(2)]
        kTh = [sb1(f"kTh{i}", [128, S], BF16) for i in range(2)]
        vh = [sb1(f"vh{i}", [128, NT * 128], BF16) for i in range(2)]
        prawk = sb1("prawk", [128, CW], BF16)
        prawq = sb1("prawq", [128, CW], BF16)
        prawv = sb1("prawv", [128, CW], BF16)
        qsw = sb1("qsw", [128, CW], BF16)
        et = [sb1(f"et{i}", [128, CW], BF16) for i in range(4)]
        lnrb = sb1("lnrb", [128, CW], F32)
        invb = [sb1(f"invbb{i}", [128, CW], F32) for i in range(2)]
        vtrash = sb1("vtrash", [128, NT * 128], BF16) if DBG_V_TRASH else None
        pA = ps("pA", [128, 512], F32)
        pB = ps("pB", [128, 512], F32)
        pC = ps("pC", [128, 512], F32)
        pD = ps("pD", [128, 512], F32)
        pE = ps("pE", [128, 512], F32)
        pF = ps("pF", [128, 512], F32)
        pG = ps("pG", [128, 512], F32)
        pH = ps("pH", [128, 512], F32)

        sems = {}
        cnt = {}
        for name in SEM_NAMES:
            sems[name] = ctx.enter_context(nc.semaphore(name))
            cnt[name] = 0
        prog = {k: [] for k in ["sync", "scalar", "vector", "tensor", "gpsimd"]}

        def on(engname):
            def deco(fn):
                prog[engname].append(fn)
                return None
            return deco

        def bump(name, amt=1):
            cnt[name] += amt
            return cnt[name]

        def dma(engname, sem_name, make_out_in, extra_waits=(), ordered=True):
            prev = cnt[sem_name] if ordered else 0
            ws = tuple(extra_waits)

            @on(engname)
            def _(e, sem_name=sem_name, prev=prev, ws=ws, make_out_in=make_out_in):
                for s_, v_ in ws:
                    if v_:
                        e.wait_ge(sems[s_], v_)
                if prev:
                    e.wait_ge(sems[sem_name], prev)
                out, in_, tp = make_out_in()
                if tp:
                    e.dma_start_transpose(out, in_).then_inc(sems[sem_name], 16)
                else:
                    e.dma_start(out=out, in_=in_).then_inc(sems[sem_name], 16)
            return bump(sem_name, 16)

        for dst, src in [(c2, c2_d), (s2, s2_d), (tri, tri_d), (onc, onc_d),
                         (oncf, oncf_d), (onr, onr_d), (onrf, onrf_d), (onm, onm_d)]:
            dma("sync", "c", (lambda dst=dst, src=src: (dst[:, :], src, False)),
                ordered=False)
        consts_c = cnt["c"]
        dma("gpsimd", "bias", (lambda: (biasb[:, :], bias_d, False)))
        bias_v = cnt["bias"]

        @on("vector")
        def _(e):
            e.memset(epsv[:, :], LN_EPS)
            e.memset(bias3[:, :], -3.0)
            e.wait_ge(sems["c"], consts_c)

        bb_banks = [pH, pE, pG, pH]
        bb_pe = [0] * 4
        for j in range(3):
            @on("tensor")
            def _(e, j=j, bv=bias_v, bk=bb_banks[j]):
                if j == 0:
                    e.wait_ge(sems["c"], consts_c)
                    e.wait_ge(sems["bias"], bv)
                e.matmul(bk[:, 0:512], onr[:, :], biasb[0:1, j * 512:(j + 1) * 512],
                         start=True, stop=True).then_inc(sems["pe"], 1)
            bb_pe[j] = bump("pe")
        bb_state = {"biasbc_ac": 0, "pH_gate": 0}

        def plan_bias_copies():
            copy_acs = [0] * 4
            for j in range(3):
                @on("scalar")
                def _(e, j=j, pe_bb=bb_pe[j]):
                    e.wait_ge(sems["pe"], pe_bb)
                    e.activation(out=biasbc[:, j * 512:(j + 1) * 512],
                                 in_=bb_banks[j][:, 0:512],
                                 func=AF.Copy).then_inc(sems["ac"], 1)
                copy_acs[j] = bump("ac")

            @on("tensor")
            def _(e, c0=copy_acs[0]):
                e.wait_ge(sems["ac"], c0)
                e.matmul(pH[:, 0:512], onr[:, :], biasb[0:1, 3 * 512:4 * 512],
                         start=True, stop=True).then_inc(sems["pe"], 1)
            bb_pe[3] = bump("pe")

            @on("scalar")
            def _(e, pe_bb=bb_pe[3]):
                e.wait_ge(sems["pe"], pe_bb)
                e.activation(out=biasbc[:, 3 * 512:4 * 512], in_=pH[:, 0:512],
                             func=AF.Copy).then_inc(sems["ac"], 1)
            copy_acs[3] = bump("ac")
            bb_state["biasbc_ac"] = copy_acs[3]
            bb_state["pH_gate"] = copy_acs[3]

        @on("tensor")
        def _(e):
            e.wait_ge(sems["c"], consts_c)

        xst_free = [("dv", 0), ("dv", 0), ("dv", 0)]
        xnn_free = [("txn", 0), ("txb", 0)]
        ln_state = {"prev_aggr": 0}
        ln_prev = [0, 0, 0]

        def plan_ln_tile(tt):
            b = tt % 3
            b2 = tt % 2
            xs = "x%d" % b
            my_x = dma("gpsimd", xs,
                       (lambda tt=tt, b=b: (xst[b][:, :],
                        bass.AP(x_d.tensor, tt * 128 * E, [[E, 128], [1, E]]), False)),
                       extra_waits=[xst_free[b]])

            @on("scalar")
            def _(e, b=b, b2=b2, xs=xs, my_x=my_x, xf=xnn_free[b2], prev=ln_prev[b]):
                e.wait_ge(sems[xs], my_x)
                if xf:
                    e.wait_ge(sems[xf[0]], xf[1])
                if prev:
                    e.wait_ge(sems["dv"], prev)
                e.activation(out=xnn[b2][:, :], in_=xst[b][:, :], func=AF.Copy,
                             accum_out=mv[b][:, 0:1]).then_inc(sems["ac"], 1)
            ac_copy = bump("ac")

            @on("scalar")
            def _(e, b=b, b2=b2, ac_copy=ac_copy):
                e.wait_ge(sems["ac"], ac_copy)
                e.activation(out=xnn[b2][:, :], in_=xst[b][:, :], func=AF.Square,
                             accum_out=mv[b][:, 1:2]).then_inc(sems["ac"], 1)
            ac_sums = bump("ac")

            dv_m = bump("dv")
            dv_var1 = bump("dv")

            @on("vector")
            def _(e, b=b, ac_sums=ac_sums, dv_m=dv_m):
                e.wait_ge(sems["ac"], ac_sums)
                e.tensor_scalar(out=mv[b][:, 0:1], in0=mv[b][:, 0:1], scalar1=1.0 / E,
                                scalar2=None, op0=ALU.mult).then_inc(sems["dv"], 1)
                e.wait_ge(sems["dv"], dv_m)
                e.scalar_tensor_tensor(out=st6[:, b:b + 1], in0=mv[b][:, 0:1], scalar=-1.0,
                                       in1=mv[b][:, 0:1], op0=ALU.mult,
                                       op1=ALU.mult).then_inc(sems["dv"], 1)
            bump_chk = cnt["dv"]
            assert bump_chk == dv_var1

            dv_var2 = bump("dv")

            @on("vector")
            def _(e, b=b, dv_var1=dv_var1):
                e.wait_ge(sems["dv"], dv_var1)
                e.tensor_scalar(out=rstd[b][:, :], in0=mv[b][:, 1:2], scalar1=1.0 / E,
                                scalar2=st6[:, b:b + 1], op0=ALU.mult,
                                op1=ALU.add).then_inc(sems["dv"], 1)

            @on("scalar")
            def _(e, b=b, dv_var2=dv_var2):
                e.wait_ge(sems["dv"], dv_var2)
                e.activation(out=rstd[b][:, :], in_=rstd[b][:, :], func=AF.Ln,
                             bias=epsv[:, :]).then_inc(sems["ac"], 1)
            ac_ln = bump("ac")

            @on("scalar")
            def _(e, b=b, ac_ln=ac_ln):
                e.wait_ge(sems["ac"], ac_ln)
                e.activation(out=rstd[b][:, :], in_=rstd[b][:, :], func=AF.Exp,
                             scale=-0.5).then_inc(sems["ac"], 1)
            ac_rstd = bump("ac")

            @on("vector")
            def _(e, b=b, b2=b2, ac_rstd=ac_rstd):
                e.wait_ge(sems["ac"], ac_rstd)
                e.tensor_scalar(out=xnn[b2][:, :], in0=xst[b][:, :], scalar1=mv[b][:, 0:1],
                                scalar2=rstd[b][:, :], op0=ALU.subtract,
                                op1=ALU.mult).then_inc(sems["dv"], 1)
            dv_xn = bump("dv")
            xst_free[b] = ("dv", dv_xn)
            ln_prev[b] = dv_xn

            txs = "txn" if tt % 2 == 0 else "txb"
            xnn_free[b2] = (txs, dma("sync", txs,
                              (lambda tt=tt, b2=b2: (
                                  bass.AP(xnT, tt * 128, [[NT * S, 128], [S, NT], [1, 128]]),
                                  xnn[b2][:, :], True)),
                              extra_waits=[("dv", dv_xn)]))

        plan_ln_tile(0)
        plan_bias_copies()
        for tt in range(1, 4):
            plan_ln_tile(tt)
        xnT_ready = 16 * (NT // 2)

        qa_ac = [0] * NT
        qb_tw = [0] * NT
        qb_sw = [0] * NT
        quantB_state = {"prev_sub": 0}

        def plan_quantA_tile(qt):
            b = qt % 3
            b2 = qt % 2
            xs = "x%d" % b
            my_x = dma("gpsimd", xs,
                       (lambda qt=qt, b=b: (xst[b][:, :],
                        bass.AP(wf_d.tensor, qt * 128 * E, [[E, 128], [1, E]]), False)),
                       extra_waits=[xst_free[b]])

            @on("scalar")
            def _(e, qt=qt, b=b, b2=b2, xs=xs, my_x=my_x):
                e.wait_ge(sems[xs], my_x)
                e.wait_ge(sems["txn"], xnT_ready)
                e.wait_ge(sems["txb"], xnT_ready)
                e.activation(out=xnn[b2][:, :], in_=xst[b][:, :], func=AF.Abs,
                             accum_out=accw[:, qt:qt + 1]).then_inc(sems["ac"], 1)
            qa_ac[qt] = bump("ac")
            xst_free[b] = ("ac", qa_ac[qt])

        nonlocal_holder = {}

        def plan_quant_reduce():
            la = qa_ac[NT - 1]

            @on("vector")
            def _(e, la=la):
                e.wait_ge(sems["ac"], la)
                e.tensor_reduce(out=accr[:, :], in_=accw[:, :], axis=mybir.AxisListType.X,
                                op=ALU.add).then_inc(sems["dv"], 1)
            dv_red = bump("dv")

            @on("tensor")
            def _(e, dv_red=dv_red, acg=cnt["ac"]):
                e.wait_ge(sems["dv"], dv_red)
                e.wait_ge(sems["ac"], acg)
                e.matmul(pF[0:1, 0:1], oncf[:, :], accr[:, :], start=True,
                         stop=True).then_inc(sems["pe"], 1)
            pe_sum = bump("pe")

            @on("vector")
            def _(e, pe_sum=pe_sum):
                e.wait_ge(sems["pe"], pe_sum)
                e.tensor_scalar(out=accr[0:1, 0:1], in0=pF[0:1, 0:1],
                                scalar1=1.0 / float(E * E), scalar2=None,
                                op0=ALU.mult).then_inc(sems["dv"], 1)
            dv_scale = bump("dv")

            @on("tensor")
            def _(e, dv_scale=dv_scale):
                e.wait_ge(sems["dv"], dv_scale)
                e.matmul(pF[:, 0:1], onrf[:, :], accr[0:1, 0:1], start=True,
                         stop=True).then_inc(sems["pe"], 1)
            pe_bc = bump("pe")

            @on("vector")
            def _(e, pe_bc=pe_bc):
                e.wait_ge(sems["pe"], pe_bc)
                e.tensor_copy(sclv[:, :], pF[:, 0:1]).then_inc(sems["dv"], 1)
            d0 = bump("dv")
            nonlocal_holder["rs_pe"] = pe_bc

            @on("vector")
            def _(e, d0=d0):
                e.wait_ge(sems["dv"], d0)
                e.tensor_scalar(out=thrv[:, :], in0=sclv[:, :], scalar1=0.5,
                                scalar2=0.5 * FFN_EPS, op0=ALU.mult,
                                op1=ALU.add).then_inc(sems["dv"], 1)
            d1 = bump("dv")

            @on("vector")
            def _(e, d1=d1):
                e.wait_ge(sems["dv"], d1)
                e.tensor_scalar(out=nthrv[:, :], in0=thrv[:, :], scalar1=-1.0,
                                scalar2=None, op0=ALU.mult).then_inc(sems["dv"], 1)
            return bump("dv")

        def plan_quantB_tile(qt, thr_dv):
            b = qt % 3
            xs = "x%d" % b
            my_x = dma("gpsimd", xs,
                       (lambda qt=qt, b=b: (xst[b][:, :],
                        bass.AP(wf_d.tensor, qt * 128 * E, [[E, 128], [1, E]]), False)),
                       extra_waits=[xst_free[b]])

            wtbf_free = (("tw%d" % ((qt - 1) % 2)), qb_tw[qt - 1]) if qt >= 1 else None
            prev_sub = quantB_state["prev_sub"]
            dv_isge = bump("dv")
            dv_isle = bump("dv")
            dv_sub = bump("dv")
            quantB_state["prev_sub"] = dv_sub

            @on("vector")
            def _(e, qt=qt, b=b, xs=xs, my_x=my_x, thr_dv=thr_dv, wtbf_free=wtbf_free,
                  dv_isge=dv_isge, dv_isle=dv_isle, prev_sub=prev_sub):
                e.wait_ge(sems[xs], my_x)
                if thr_dv:
                    e.wait_ge(sems["dv"], thr_dv)
                if prev_sub:
                    e.wait_ge(sems["dv"], prev_sub)
                e.tensor_scalar(out=xnn[0][:, :], in0=xst[b][:, :], scalar1=thrv[:, :],
                                scalar2=sclv[:, :], op0=ALU.is_ge,
                                op1=ALU.mult).then_inc(sems["dv"], 1)
                e.wait_ge(sems["dv"], dv_isge)
                e.tensor_scalar(out=xnn[1][:, :], in0=xst[b][:, :], scalar1=nthrv[:, :],
                                scalar2=sclv[:, :], op0=ALU.is_le,
                                op1=ALU.mult).then_inc(sems["dv"], 1)
                e.wait_ge(sems["dv"], dv_isle)
                if wtbf_free:
                    e.wait_ge(sems[wtbf_free[0]], wtbf_free[1])
                e.tensor_tensor(out=wtbf[b][:, :], in0=xnn[0][:, :], in1=xnn[1][:, :],
                                op=ALU.subtract).then_inc(sems["dv"], 1)
            xst_free[b] = ("dv", dv_isle)

            tw = "tw%d" % (qt % 2)
            sw = "sw%d" % (qt % 2)
            wtTs_free = (("sw%d" % ((qt - 1) % 2)), qb_sw[qt - 1]) if qt >= 1 else None
            qb_tw[qt] = dma("sync", tw,
                            (lambda b=b: (
                                bass.AP(wtTs[b], 0, [[NT * 128, 128], [128, NT], [1, 128]]),
                                wtbf[b][:, :], True)),
                            extra_waits=[("dv", dv_sub)] +
                                        ([wtTs_free] if wtTs_free else []))
            qb_sw[qt] = dma("sync", sw,
                            (lambda qt=qt, b=b: (
                                bass.AP(wtT_d.tensor, qt * 128, [[E, 128], [128 * E, NT], [1, 128]]),
                                bass.AP(wtTs[b], 0, [[NT * 128, 128], [128, NT], [1, 128]]), False)),
                            extra_waits=[(tw, qb_tw[qt])])

        def plan_head_weights(h, gate_pe):
            out = {}
            hb = h % 2
            for wd, dsts, s_ in ((wq_d, wslq, "wq"), (wk_d, wslk, "wk"), (wv_d, wslv, "wv")):
                if DBG_NO_WSL:
                    out[s_] = cnt[s_]
                    continue
                out[s_] = dma("gpsimd", s_,
                              (lambda wd=wd, dst=dsts[hb], h=h: (dst[:, :],
                               bass.AP(wd.tensor, h * 128, [[E, 128], [E * 128, NT], [1, 128]]),
                               False)),
                              extra_waits=[("pe", gate_pe)])
            return out

        n_heads = H
        do_quant = True
        do_ph2 = True
        if TRUNC == "p0":
            n_heads = 0
            do_quant = False
            do_ph2 = False
        elif TRUNC.startswith("h"):
            n_heads = int(TRUNC[1:])
            do_ph2 = False
        if DBG_NO_QUANT:
            do_quant = False

        wsl_ready = [None] * H
        quant_thr_dv = [0]
        if n_heads:
            wsl_ready[0] = plan_head_weights(0, 0)
        head_proj_done = [0] * H
        oTs_spl = [0, 0]
        qkv_used_pe = [0, 0]
        proj_psum = [pA, pB]
        proj_parity = 0
        proj_evac_ac = [0, 0]
        praw_state = {"k": ("dv", 0), "q": ("dv", 0), "v": ("tv0", 0)}
        pv_banks = [pE, pG]
        pv_gate_dv = [0, 0]
        gchunk = [0]
        rs_gate_ac = [0]
        rs_gate_pe_extra = [0]
        prev_lnrb_ac = [0]
        prev_invb_dv = [0, 0]


        et_slot_pe = [0] * 4
        qsw_free_dv = 0
        prev_norm_dv = 0

        def make_head(h):
            hb = h % 2
            tvh = "tv%d" % hb
            soh = "so%d" % hb
            ch_k = [0] * CH
            ch_q = [0] * CH
            ch_v = [0] * CH

            def plan_proj_chunk(c, h=h, hb=hb, tvh=tvh):
                nonlocal proj_parity, qsw_free_dv
                cs = slice(c * CW, (c + 1) * CW)
                for which, wsls, pr, key, wsem in ((0, wslk, prawk, "k", "wk"),
                                                   (1, wslq, prawq, "q", "wq"),
                                                   (2, wslv, prawv, "v", "wv")):
                    wsl = wsls[hb]
                    pp = proj_psum[proj_parity]
                    free_ac = proj_evac_ac[proj_parity]

                    @on("tensor")
                    def _(e, wsl=wsl, pp=pp, c=c, wsem=wsem, ldw_gate=wsl_ready[h][wsem],
                          free_ac=free_ac,
                          txn_gate=(16 * 2 * (c + 1)) if h == 0 else 0):
                        e.wait_ge(sems[wsem], ldw_gate)
                        if txn_gate:
                            e.wait_ge(sems["txn"], txn_gate)
                            e.wait_ge(sems["txb"], txn_gate)
                        if free_ac:
                            e.wait_ge(sems["dv"], free_ac)
                        for it in range(NT - 1):
                            e.matmul(pp[:, 0:CW], wsl[:, it * 128:(it + 1) * 128],
                                     bass.AP(xnT, it * S + c * CW, [[NT * S, 128], [1, CW]]),
                                     start=(it == 0), stop=False)
                        e.matmul(pp[:, 0:CW], wsl[:, (NT - 1) * 128:NT * 128],
                                 bass.AP(xnT, (NT - 1) * S + c * CW, [[NT * S, 128], [1, CW]]),
                                 start=False, stop=True).then_inc(sems["pe"], 1)
                    pe_proj = bump("pe")

                    psem, pval = praw_state[key]

                    @on("vector")
                    def _(e, pp=pp, pr=pr, pe_proj=pe_proj, psem=psem, pval=pval):
                        e.wait_ge(sems["pe"], pe_proj)
                        if pval:
                            e.wait_ge(sems[psem], pval)
                        e.tensor_copy(pr[:, :], pp[:, 0:CW]).then_inc(sems["dv"], 1)
                    dv_evac = bump("dv")
                    proj_evac_ac[proj_parity] = dv_evac
                    proj_parity ^= 1

                    if which < 2:
                        dstT = kTh[hb] if which == 0 else qTh[hb]
                        dv_swap = bump("dv")
                        dv_mul = bump("dv")
                        dv_rope = bump("dv")

                        @on("vector")
                        def _(e, pr=pr, dstT=dstT, cs=cs, dv_evac=dv_evac,
                              qk_gate=qkv_used_pe[hb] if (h >= 2 and c == 0 and which == 0) else 0,
                              dv_swap=dv_swap, dv_mul=dv_mul, qsw_free=qsw_free_dv):
                            e.wait_ge(sems["dv"], dv_evac)
                            if qk_gate:
                                e.wait_ge(sems["pe"], qk_gate)
                            if qsw_free:
                                e.wait_ge(sems["dv"], qsw_free)
                            e.tensor_copy(qsw[0:64, :], pr[64:128, :])
                            e.tensor_copy(qsw[64:128, :], pr[0:64, :]).then_inc(sems["dv"], 1)
                            e.wait_ge(sems["dv"], dv_swap)
                            e.tensor_tensor(out=dstT[:, cs], in0=pr[:, :], in1=c2[:, cs], op=ALU.mult)
                            e.tensor_tensor(out=qsw[:, :], in0=qsw[:, :], in1=s2[:, cs],
                                            op=ALU.mult).then_inc(sems["dv"], 1)
                            e.wait_ge(sems["dv"], dv_mul)
                            e.tensor_tensor(out=dstT[:, cs], in0=dstT[:, cs], in1=qsw[:, :],
                                            op=ALU.add).then_inc(sems["dv"], 1)
                        praw_state[key] = ("dv", dv_mul)
                        qsw_free_dv = dv_rope
                        if which == 0:
                            ch_k[c] = dv_rope
                        else:
                            ch_q[c] = dv_rope
                    else:
                        ew = [("dv", dv_evac)]
                        if h >= 2 and c == 0:
                            ew.append(("pe", qkv_used_pe[hb]))
                        vdst = vtrash if DBG_V_TRASH else vh[hb]
                        ch_v[c] = dma("sync", tvh,
                                      (lambda pr=pr, vdst=vdst, c=c: (
                                          bass.AP(vdst, 4 * c * 128, [[NT * 128, 128], [128, 4], [1, 128]]),
                                          pr[:, :], True)),
                                      extra_waits=ew)
                        praw_state["v"] = (tvh, ch_v[c])
                if c == CH - 1:
                    head_proj_done[h] = cnt["pe"]

            def plan_att_chunk(c, h=h, hb=hb, tvh=tvh, soh=soh):
                nonlocal prev_norm_dv
                gp = gchunk[0] % 2
                gchunk[0] += 1
                pv = pv_banks[gp]
                cs = slice(c * CW, (c + 1) * CW)
                nkt = 4 * c + 4
                sc_state = {"parity": 0}
                sc_psums = [pC, pD, pH]
                sc_free_ac = [0, 0, 0]
                exp_acs = [0] * nkt
                mask_dvs = [0] * nkt
                kdv, qdv, vr = ch_k[c], ch_q[c], ch_v[c]

                def plan_scores(kt):
                    col0 = max(0, (kt - 4 * c) * 128)
                    n = CW - col0
                    pc_ = sc_psums[sc_state["parity"]]
                    my_free = sc_free_ac[sc_state["parity"]]
                    if my_free == 0 and pc_ is pH and bb_state["pH_gate"]:
                        my_free = bb_state["pH_gate"]
                        bb_state["pH_gate"] = 0

                    @on("tensor")
                    def _(e, pc_=pc_, kt=kt, col0=col0, n=n, my_free=my_free,
                          kdv=kdv, qdv=qdv, cs0=c * CW):
                        e.wait_ge(sems["dv"], max(kdv, qdv))
                        if my_free:
                            e.wait_ge(sems["ac"], my_free)
                        e.matmul(pc_[:, col0:CW], kTh[hb][:, kt * 128:(kt + 1) * 128],
                                 bass.AP(qTh[hb], cs0 + col0, [[S, 128], [1, n]]),
                                 start=True, stop=True).then_inc(sems["pe"], 1)
                    pe_sc = bump("pe")

                    slot = kt % 4
                    et_gate = et_slot_pe[slot]

                    @on("scalar")
                    def _(e, pc_=pc_, col0=col0, slot=slot, pe_sc=pe_sc, et_gate=et_gate):
                        e.wait_ge(sems["pe"], pe_sc)
                        if et_gate:
                            e.wait_ge(sems["pe"], et_gate)
                        e.activation(out=et[slot][:, col0:CW], in_=pc_[:, col0:CW],
                                     func=AF.Exp, scale=INV_SQRT_D,
                                     bias=bias3[:, :]).then_inc(sems["ac"], 1)
                    exp_acs[kt] = bump("ac")
                    sc_free_ac[sc_state["parity"]] = exp_acs[kt]
                    sc_state["parity"] = (sc_state["parity"] + 1) % 3

                    if kt >= 4 * c:
                        @on("vector")
                        def _(e, slot=slot, col0=col0, ac_exp=exp_acs[kt]):
                            e.wait_ge(sems["ac"], ac_exp)
                            e.tensor_tensor(out=et[slot][:, col0:col0 + 128],
                                            in0=et[slot][:, col0:col0 + 128],
                                            in1=tri[:, :], op=ALU.mult).then_inc(sems["dv"], 1)
                        mask_dvs[kt] = bump("dv")

                def plan_rspv(kt):
                    col0 = max(0, (kt - 4 * c) * 128)
                    slot = kt % 4

                    @on("tensor")
                    def _(e, kt=kt, col0=col0, slot=slot, nkt=nkt, tvh=tvh, pv=pv,
                          ac_exp=exp_acs[kt], mdv=mask_dvs[kt],
                          rs_gate=rs_gate_ac[0] if kt == 0 else 0,
                          rs_pe=rs_gate_pe_extra[0] if kt == 0 else 0,
                          pv_gate=pv_gate_dv[gp] if kt == 0 else 0, vr=vr):
                        e.wait_ge(sems["ac"], ac_exp)
                        if mdv:
                            e.wait_ge(sems["dv"], mdv)
                        if kt == 0:
                            if rs_gate:
                                e.wait_ge(sems["ac"], rs_gate)
                            if rs_pe:
                                e.wait_ge(sems["pe"], rs_pe)
                            if pv_gate:
                                e.wait_ge(sems["dv"], pv_gate)
                            e.wait_ge(sems[tvh], vr)
                        e.matmul(pF[:, col0:CW], onm[:, :], et[slot][:, col0:CW],
                                 start=(kt == 0), stop=(kt == nkt - 1))
                        e.matmul(pv[:, col0:CW], vh[hb][:, kt * 128:(kt + 1) * 128],
                                 et[slot][:, col0:CW],
                                 start=(kt == 0), stop=(kt == nkt - 1)).then_inc(sems["pe"], 1)
                    et_slot_pe[slot] = bump("pe")

                for kt in range(nkt):
                    plan_scores(kt)
                    if kt >= 3:
                        plan_rspv(kt - 3)
                for kt in range(max(0, nkt - 3), nkt):
                    plan_rspv(kt)
                pe_pv_done = cnt["pe"]

                @on("scalar")
                def _(e, pe_pv=pe_pv_done, pln=prev_lnrb_ac[0]):
                    e.wait_ge(sems["pe"], pe_pv)
                    if pln:
                        e.wait_ge(sems["ac"], pln)
                    e.activation(out=lnrb[:, :], in_=pF[:, 0:CW], func=AF.Ln).then_inc(sems["ac"], 1)
                ac_lnr = bump("ac")
                rs_gate_ac[0] = ac_lnr

                @on("scalar")
                def _(e, ac_lnr=ac_lnr, gp=gp, pnv=prev_invb_dv[gp]):
                    e.wait_ge(sems["ac"], ac_lnr)
                    if pnv:
                        e.wait_ge(sems["dv"], pnv)
                    e.activation(out=invb[gp][:, :], in_=lnrb[:, :], func=AF.Exp,
                                 scale=-1.0).then_inc(sems["ac"], 1)
                ac_inv = bump("ac")
                prev_lnrb_ac[0] = ac_inv

                def plan_tail(ac_inv=ac_inv, cs=cs, c=c, gp=gp, pv=pv):
                    nonlocal prev_norm_dv
                    dv_norm = bump("dv")

                    @on("vector")
                    def _(e, ac_inv=ac_inv, cs=cs, pv=pv, gp=gp,
                          spl_gate=oTs_spl[hb] if (h >= 2 and c == 0) else 0,
                          prev_norm=prev_norm_dv):
                        e.wait_ge(sems["ac"], ac_inv)
                        if prev_norm:
                            e.wait_ge(sems["dv"], prev_norm)
                        if spl_gate:
                            e.wait_ge(sems[soh], spl_gate)
                        e.tensor_tensor(out=oTs[hb][:, cs], in0=pv[:, 0:CW], in1=invb[gp][:, :],
                                        op=ALU.mult).then_inc(sems["dv"], 1)
                    prev_invb_dv[gp] = dv_norm
                    pv_gate_dv[gp] = dv_norm
                    prev_norm_dv = dv_norm
                return plan_tail

            def plan_background(c, h=h):
                if h == 0 and c < CH - 1:
                    for t2 in range(4 * (c + 1), 4 * (c + 1) + 4):
                        plan_ln_tile(t2)
                if do_quant and n_heads >= 14:
                    if 2 <= h <= 5:
                        plan_quantA_tile(4 * (h - 2) + c)
                    elif h == 6 and c == 0:
                        quant_thr_dv[0] = plan_quant_reduce()
                        rs_gate_pe_extra[0] = nonlocal_holder.get("rs_pe", 0)
                        plan_quantB_tile(0, quant_thr_dv[0])
                    elif 6 <= h <= 13 and c % 2 == 0 and not (h == 6 and c == 0):
                        plan_quantB_tile(2 * (h - 6) + c // 2, 0)

            def finish_head():
                qkv_used_pe[hb] = cnt["pe"]
                if h < H - 2:
                    oTs_spl[hb] = dma("sync", soh,
                                      (lambda h=h, hb=hb: (
                                          bass.AP(outT_d.tensor, h * 128 * S, [[S, 128], [1, S]]),
                                          oTs[hb][:, :], False)),
                                      extra_waits=[("dv", prev_norm_dv)])

            return plan_proj_chunk, plan_att_chunk, plan_background, finish_head

        if n_heads >= 2:
            p0, a0, b0, f0 = make_head(0)
            p1, a1, b1, f1 = make_head(1)
            wsl_ready[1] = plan_head_weights(1, 0)
            p0(0)
            p1(0)
            b0(0)
            p0(1)
            t = a0(0)
            p1(1)
            b0(1)
            p0(2)
            t()
            t = a0(1)
            p1(2)
            b0(2)
            p0(3)
            t()
            t = a0(2)
            p1(3)
            t()
            t = a0(3)
            t()
            t = a1(0)
            t()
            t = a1(1)
            t()
            f0()
            t = a1(2)
            t()
            t = a1(3)
            t()
            f1()
            if n_heads > 2:
                wsl_ready[2] = plan_head_weights(2, head_proj_done[0])
        elif n_heads == 1:
            p0, a0, b0, f0 = make_head(0)
            p0(0)
            b0(0)
            pend_tail = None
            for c in range(1, CH):
                p0(c)
                if pend_tail is not None:
                    pend_tail()
                pend_tail = a0(c - 1)
                b0(c)
            pend_tail()
            pend_tail = a0(CH - 1)
            pend_tail()
            f0()

        for h in range(2, n_heads):
            plan_proj_chunk, plan_att_chunk, plan_background, finish_head = make_head(h)
            plan_proj_chunk(0)
            pend_tail = None
            for c in range(1, CH):
                plan_proj_chunk(c)
                if pend_tail is not None:
                    pend_tail()
                pend_tail = plan_att_chunk(c - 1)
                if c == 1:
                    plan_background(0)
                if c < CH - 1:
                    plan_background(c)
            pend_tail()
            pend_tail = plan_att_chunk(CH - 1)
            pend_tail()
            plan_background(CH - 1)
            finish_head()
            if h + 1 < n_heads:
                wsl_ready[h + 1] = plan_head_weights(h + 1, head_proj_done[h - 1])

        if n_heads == 0:
            for t2 in range(4, NT):
                plan_ln_tile(t2)
        if do_quant and n_heads < 14:
            if n_heads <= 6:
                nA = max(0, (min(n_heads, 6) - 2) * 4)
                for qt in range(nA, NT):
                    plan_quantA_tile(qt)
                thr_dv = plan_quant_reduce()
                for qt in range(NT):
                    plan_quantB_tile(qt, thr_dv if qt < 4 else 0)
            else:
                for qt in range(2 * (n_heads - 6), NT):
                    plan_quantB_tile(qt, 0)

        so_final = [cnt["so0"], cnt["so1"]]
        sw_final = [cnt["sw0"], cnt["sw1"]]
        ph2_finals = [("so0", so_final[0]), ("so1", so_final[1]),
                      ("sw0", sw_final[0]), ("sw1", sw_final[1]),
                      ("tv0", cnt["tv0"]), ("tv1", cnt["tv1"]),
                      ("tw0", cnt["tw0"]), ("tw1", cnt["tw1"]),
                      ("txn", cnt["txn"]), ("txb", cnt["txb"]),
                      ("pe", cnt["pe"]), ("dv", cnt["dv"])]

        with nc.Block("phase01") as block:
            @block.sync
            def _(e):
                for f in prog["sync"]:
                    f(e)

            @block.scalar
            def _(e):
                for f in prog["scalar"]:
                    f(e)

            @block.vector
            def _(e):
                for f in prog["vector"]:
                    f(e)

            @block.tensor
            def _(e):
                for f in prog["tensor"]:
                    f(e)

            @block.gpsimd
            def _(e):
                for f in prog["gpsimd"]:
                    f(e)

        ctx1.close()
        oTh = [ctx.enter_context(nc.sbuf_tensor(f"oTh{i}", [128, H * 1024], BF16))
               for i in range(2)]
        woc = [ctx.enter_context(nc.sbuf_tensor(f"woc{i}", [128, NT * W2], BF16)) for i in range(2)]
        wtc = [ctx.enter_context(nc.sbuf_tensor(f"wtc{i}", [128, NT * W2], BF16)) for i in range(2)]
        xsl = [ctx.enter_context(nc.sbuf_tensor(f"xsl{i}", [128, W2], F32)) for i in range(2)]
        osl = [ctx.enter_context(nc.sbuf_tensor(f"osl{i}", [128, W2], F32)) for i in range(2)]
        prog = {k: [] for k in ["sync", "scalar", "vector", "tensor", "gpsimd"]}

        if DEBUG_DUMP:
            @on("gpsimd")
            def _(e, xr=xnT_ready, bv=cnt["bias"]):
                e.wait_ge(sems["txn"], xr)
                e.wait_ge(sems["txb"], xr)
                e.wait_ge(sems["bias"], bv)
                e.dma_start(out=xnT_d, in_=xnT[:, :]).then_inc(sems["bias"], 16)
            bump("bias", 16)

        ph2_psums = [pA, pB]
        ph2_free_dv = [0, 0]
        woc_used_pe = [0, 0]
        wtc_used_pe = [0, 0]
        xsl_used_dv = [0, 0]
        osl_st = [0, 0]

        chunks = [(half, ec) for half in range(2 if do_ph2 else 0) for ec in range(NEC)]
        wld = {}

        def plan_ph2_weights(idx):
            half, ec = chunks[idx]
            ecb = idx % 2
            wop = "wo%d" % ecb
            wcp = "wc%d" % ecb
            first = idx < 2
            woc_ld = dma("gpsimd", wop,
                         (lambda ec=ec, ecb=ecb: (woc[ecb][:, :],
                          bass.AP(wo_d.tensor, ec * W2, [[E, 128], [E * 128, NT], [1, W2]]),
                          False)),
                         extra_waits=(ph2_finals if first else []) +
                                     [("pe", woc_used_pe[ecb])])
            wtc_ld = dma("scalar", wcp,
                         (lambda ec=ec, ecb=ecb: (wtc[ecb][:, :],
                          bass.AP(wtT_d.tensor, ec * W2, [[E, 128], [128 * E, NT], [1, W2]]),
                          False)),
                         extra_waits=(ph2_finals if first else
                                      [("sw0", sw_final[0]), ("sw1", sw_final[1])]) +
                                     [("pe", wtc_used_pe[ecb])])
            wld[idx] = (woc_ld, wtc_ld)

        if chunks:
            plan_ph2_weights(0)
            plan_ph2_weights(1)
        xsl_pre = {}
        for tb in range(2 if do_ph2 else 0):
            xsl_pre[tb] = dma("sync", "xs%d" % tb,
                              (lambda tb=tb: (xsl[tb][:, :],
                               bass.AP(x_d.tensor, tb * 128 * E, [[E, 128], [1, W2]]),
                               False)),
                              extra_waits=ph2_finals)

        if do_ph2:
            for hh in range(H - 2):
                dma("sync", OTH0[hh % 4],
                    (lambda hh=hh: (
                        bass.AP(oTh[0], hh * 1024, [[H * 1024, 128], [1, 1024]]),
                        bass.AP(outT_d.tensor, hh * 128 * S, [[S, 128], [1, 1024]]),
                        False)),
                    extra_waits=(ph2_finals if hh < 4 else []))
            for hh in range(H - 2):
                dma("scalar", "oth1",
                    (lambda hh=hh: (
                        bass.AP(oTh[1], hh * 1024, [[H * 1024, 128], [1, 1024]]),
                        bass.AP(outT_d.tensor, hh * 128 * S + 1024, [[S, 128], [1, 1024]]),
                        False)),
                    extra_waits=(ph2_finals if hh == 0 else []))
        oTh0_ld = [cnt[s] for s in OTH0]
        oTh1_ld = cnt["oth1"]
        for idx in range(len(chunks)):
            half, ec = chunks[idx]
            ecb = idx % 2
            wop = "wo%d" % ecb
            wcp = "wc%d" % ecb
            if idx + 1 < len(chunks) and idx + 1 not in wld:
                plan_ph2_weights(idx + 1)
            woc_ld, wtc_ld = wld[idx]
            if True:

                for ti in range(8):
                    tt = half * 8 + ti
                    tb = tt % 2
                    xsp = "xs%d" % tb
                    syp = "sy%d" % tb
                    pp = ph2_psums[tb]
                    xsl_first = (half == 0 and ec == 0 and ti < 2)
                    if xsl_first:
                        xsl_ld = xsl_pre[tb]
                    else:
                        xsl_ld = dma("sync", xsp,
                                     (lambda tt=tt, tb=tb, ec=ec: (xsl[tb][:, :],
                                      bass.AP(x_d.tensor, tt * 128 * E + ec * W2, [[E, 128], [1, W2]]),
                                      False)),
                                     extra_waits=[("dv", xsl_used_dv[tb])])

                    @on("tensor")
                    def _(e, pp=pp, ti=ti, tt=tt, ec=ec, ecb=ecb, wop=wop, wcp=wcp,
                          half=half, woc_ld=woc_ld, wtc_ld=wtc_ld,
                          dv_gate=ph2_free_dv[tb], bias_v=bias_v):
                        e.wait_ge(sems[wcp], wtc_ld)
                        e.wait_ge(sems["bias"], bias_v)
                        if dv_gate:
                            e.wait_ge(sems["dv"], dv_gate)
                        for it in range(NT):
                            e.matmul(pp[:, 0:W2],
                                     bass.AP(xnT, it * S + tt * 128, [[NT * S, 128], [1, 128]]),
                                     wtc[ecb][:, it * W2:(it + 1) * W2],
                                     start=(it == 0), stop=False)
                        e.wait_ge(sems[wop], woc_ld)
                        if half == 0:
                            for s_, v_ in zip(OTH0, oTh0_ld):
                                e.wait_ge(sems[s_], v_)
                        else:
                            e.wait_ge(sems["oth1"], oTh1_ld)
                        for it in range(NT):
                            if it >= H - 2:
                                lh = oTs[it % 2][:, tt * 128:(tt + 1) * 128]
                            else:
                                lh = bass.AP(oTh[half], it * 1024 + ti * 128,
                                             [[H * 1024, 128], [1, 128]])
                            mm = e.matmul(pp[:, 0:W2], lh,
                                          woc[ecb][:, it * W2:(it + 1) * W2],
                                          start=False, stop=(it == NT - 1))
                            if it == NT - 1:
                                mm.then_inc(sems["pe"], 1)
                    pe_out = bump("pe")
                    woc_used_pe[ecb] = pe_out
                    wtc_used_pe[ecb] = pe_out


                    dv_out1 = bump("dv")

                    @on("vector")
                    def _(e, pp=pp, tb=tb, ec=ec, xsp=xsp, syp=syp, pe_out=pe_out,
                          xsl_ld=xsl_ld, osl_gate=osl_st[tb], dv_out1=dv_out1,
                          bac=bb_state["biasbc_ac"]):
                        e.wait_ge(sems["pe"], pe_out)
                        e.wait_ge(sems[xsp], xsl_ld)
                        e.wait_ge(sems["ac"], bac)
                        if osl_gate:
                            e.wait_ge(sems[syp], osl_gate)
                        e.tensor_tensor(out=osl[tb][:, :], in0=pp[:, 0:W2], in1=xsl[tb][:, :],
                                        op=ALU.add).then_inc(sems["dv"], 1)
                        e.wait_ge(sems["dv"], dv_out1)
                        e.tensor_tensor(out=osl[tb][:, :], in0=osl[tb][:, :],
                                        in1=biasbc[:, ec * W2:(ec + 1) * W2],
                                        op=ALU.add).then_inc(sems["dv"], 1)
                    dv_out = bump("dv")
                    ph2_free_dv[tb] = dv_out
                    xsl_used_dv[tb] = dv_out

                    osl_st[tb] = dma("scalar", syp,
                                     (lambda tt=tt, tb=tb, ec=ec: (
                                         bass.AP(y_d.tensor, tt * 128 * E + ec * W2, [[E, 128], [1, W2]]),
                                         osl[tb][:, :], False)),
                                     extra_waits=[("dv", dv_out)])

        sy_final = [cnt["sy0"], cnt["sy1"]]

        @on("sync")
        def _(e, sy_final=tuple(sy_final)):
            e.wait_ge(sems["sy0"], sy_final[0])
            e.wait_ge(sems["sy1"], sy_final[1])

        @on("scalar")
        def _(e, sy_final=tuple(sy_final)):
            e.wait_ge(sems["sy0"], sy_final[0])
            e.wait_ge(sems["sy1"], sy_final[1])

        with nc.Block("phase2") as block:
            @block.sync
            def _(e):
                for f in prog["sync"]:
                    f(e)

            @block.scalar
            def _(e):
                for f in prog["scalar"]:
                    f(e)

            @block.vector
            def _(e):
                for f in prog["vector"]:
                    f(e)

            @block.tensor
            def _(e):
                for f in prog["tensor"]:
                    f(e)

            @block.gpsimd
            def _(e):
                for f in prog["gpsimd"]:
                    f(e)

    return nc


def _consts():
    pos = np.arange(S, dtype=np.float64)
    invf = 1.0 / (10000.0 ** (np.arange(0, D, 2, dtype=np.float64) / D))
    ang = invf[:, None] * pos[None, :]
    cos = np.cos(ang).astype(np.float32)
    sin = np.sin(ang).astype(np.float32)
    c2 = np.concatenate([cos, cos], axis=0)
    s2 = np.concatenate([-sin, sin], axis=0)
    tri = np.triu(np.ones((128, 128), np.float32)).astype(BF)
    onc = np.ones((128, 1), BF)
    oncf = np.ones((128, 1), np.float32)
    onr = np.ones((1, 128), BF)
    onrf = np.ones((1, 128), np.float32)
    onm = np.ones((128, 128), BF)
    return c2, s2, tri, onc, oncf, onr, onrf, onm


def kernel(L0_attn_x, L0_attn_wq, L0_attn_wk, L0_attn_wv, L0_attn_wo,
           L0_ffn_weight, L0_ffn_bias):
    x = np.ascontiguousarray(np.asarray(L0_attn_x, dtype=np.float32))
    wq = np.ascontiguousarray(np.asarray(L0_attn_wq, dtype=np.float32))
    wk = np.ascontiguousarray(np.asarray(L0_attn_wk, dtype=np.float32))
    wv = np.ascontiguousarray(np.asarray(L0_attn_wv, dtype=np.float32))
    wo = np.ascontiguousarray(np.asarray(L0_attn_wo, dtype=np.float32))
    wf = np.ascontiguousarray(np.asarray(L0_ffn_weight, dtype=np.float32))
    fb = np.ascontiguousarray(np.asarray(L0_ffn_bias, dtype=np.float32)).reshape(1, E)
    c2, s2, tri, onc, oncf, onr, onrf, onm = _consts()
    nc = build_nc()
    in_maps = []
    for b in range(8):
        in_maps.append({
            "x": x[b], "wq": wq, "wk": wk, "wv": wv, "wo": wo, "wf": wf,
            "fbias": fb, "c2t": c2, "s2t": s2, "tri": tri, "onc": onc,
            "oncf": oncf, "onr": onr, "onrf": onrf, "onm": onm,
        })
    res = run_bass_kernel_spmd(nc, in_maps, core_ids=list(range(8)))
    out = np.stack([np.asarray(r["y"], dtype=np.float32) for r in res.results], axis=0)
    return out


# version 25 (final)
# speedup vs baseline: 1.0787x; 1.0787x over previous
import math
import numpy as np
import ml_dtypes
import concourse.bass as bass
import concourse.mybir as mybir
from concourse.bass_utils import run_bass_kernel_spmd
from contextlib import ExitStack

F32 = mybir.dt.float32
BF16 = mybir.dt.bfloat16
AF = mybir.ActivationFunctionType
ALU = mybir.AluOpType
BF = ml_dtypes.bfloat16

E = 2048
S = 2048
H = 16
D = 128
NT = 16
CH = 4
CW = 512
W2 = 256
NEC = E // W2
INV_SQRT_D = 1.0 / math.sqrt(128.0)
LN_EPS = 1e-5
FFN_EPS = 1e-10
DEBUG_DUMP = False
TRUNC = ""
DBG_NO_QUANT = False
DBG_V_TRASH = False
DBG_NO_WSL = False

SEM_NAMES = ["c", "bias", "x0", "x1", "x2", "x3", "wq", "wk", "wv", "txn", "txb", "tv0", "tv1",
             "tw0", "tw1", "so0", "so1", "sw0", "sw1", "oth0", "o0b", "o0c", "o0d",
             "oth1", "wc0", "wc1",
             "wo0", "wo1", "xs0", "xs1", "sy0", "sy1", "dv", "ac", "pe"]
OTH0 = ["oth0", "o0b", "o0c", "o0d"]


def build_nc():
    nc = bass.Bass("TRN2", target_bir_lowering=False, debug=False)
    x_d = nc.dram_tensor("x", [S, E], F32, kind="ExternalInput").ap()
    wq_d = nc.dram_tensor("wq", [E, E], F32, kind="ExternalInput").ap()
    wk_d = nc.dram_tensor("wk", [E, E], F32, kind="ExternalInput").ap()
    wv_d = nc.dram_tensor("wv", [E, E], F32, kind="ExternalInput").ap()
    wo_d = nc.dram_tensor("wo", [E, E], F32, kind="ExternalInput").ap()
    wf_d = nc.dram_tensor("wf", [E, E], F32, kind="ExternalInput").ap()
    bias_d = nc.dram_tensor("fbias", [1, E], F32, kind="ExternalInput").ap()
    c2_d = nc.dram_tensor("c2t", [128, S], F32, kind="ExternalInput").ap()
    s2_d = nc.dram_tensor("s2t", [128, S], F32, kind="ExternalInput").ap()
    tri_d = nc.dram_tensor("tri", [128, 128], BF16, kind="ExternalInput").ap()
    onc_d = nc.dram_tensor("onc", [128, 1], BF16, kind="ExternalInput").ap()
    oncf_d = nc.dram_tensor("oncf", [128, 1], F32, kind="ExternalInput").ap()
    onr_d = nc.dram_tensor("onr", [1, 128], BF16, kind="ExternalInput").ap()
    onm_d = nc.dram_tensor("onm", [128, 128], BF16, kind="ExternalInput").ap()
    onrf_d = nc.dram_tensor("onrf", [1, 128], F32, kind="ExternalInput").ap()
    y_d = nc.dram_tensor("y", [S, E], F32, kind="ExternalOutput").ap()
    scr_kind = "ExternalOutput" if DEBUG_DUMP else "Internal"
    outT_d = nc.dram_tensor("outT_scr", [H, 128, S], BF16, kind=scr_kind).ap()
    wtT_d = nc.dram_tensor("wtT_scr", [NT, 128, E], BF16, kind=scr_kind).ap()
    xnT_d = (nc.dram_tensor("xnT_dump", [128, NT * S], BF16, kind="ExternalOutput").ap()
             if DEBUG_DUMP else None)

    ctx = ExitStack()
    sb = lambda n_, s_, d_: ctx.enter_context(nc.sbuf_tensor(n_, s_, d_))
    ps = lambda n_, s_, d_: ctx.enter_context(nc.psum_tensor(n_, s_, d_))
    with ctx:
        xnT = sb("xnT", [128, NT * S], BF16)
        c2 = sb("c2", [128, S], F32)
        s2 = sb("s2", [128, S], F32)
        tri = sb("trim", [128, 128], BF16)
        onc = sb("oncs", [128, 1], BF16)
        oncf = sb("oncfs", [128, 1], F32)
        onr = sb("onrs", [1, 128], BF16)
        onm = sb("onms", [128, 128], BF16)
        onrf = sb("onrfs", [1, 128], F32)
        biasb = sb("biasb", [1, E], BF16)
        st6 = sb("st6", [128, 24], F32)
        mv = [sb(f"mv{i}", [128, 2], F32) for i in range(3)]
        rstd = [sb(f"rstd{i}", [128, 1], F32) for i in range(3)]
        epsv = sb("epsv", [128, 1], F32)
        bias3 = sb("bias3", [128, 1], F32)
        accw = sb("accw", [128, 16], F32)
        accr = sb("accr", [128, 1], F32)
        sclv = sb("sclv", [128, 1], F32)
        thrv = sb("thrv", [128, 1], F32)
        nthrv = sb("nthrv", [128, 1], F32)
        oTs = [sb(f"oTs{i}", [128, S], BF16) for i in range(2)]
        biasbc = sb("biasbc", [128, E], BF16)
        ctx1 = ExitStack()
        sb1 = lambda n_, s_, d_: ctx1.enter_context(nc.sbuf_tensor(n_, s_, d_))
        xst = [sb1(f"xst{i}", [128, E], F32) for i in range(3)]
        xnn = [sb1(f"xnn{i}", [128, E], BF16) for i in range(2)]
        wtbf = [sb1("wtbf0", [128, E], BF16)] * 4
        wtTs = [sb1("wtTs0", [128, NT * 128], BF16)] * 4
        wslq = [sb1(f"wslq{i}", [128, NT * 128], BF16) for i in range(2)]
        wslk = [sb1(f"wslk{i}", [128, NT * 128], BF16) for i in range(2)]
        wslv = [sb1(f"wslv{i}", [128, NT * 128], BF16) for i in range(2)]
        qTh = [sb1(f"qTh{i}", [128, S], BF16) for i in range(2)]
        kTh = [sb1(f"kTh{i}", [128, S], BF16) for i in range(2)]
        vh = [sb1(f"vh{i}", [128, NT * 128], BF16) for i in range(2)]
        prawk = sb1("prawk", [128, CW], BF16)
        prawq = sb1("prawq", [128, CW], BF16)
        prawv = sb1("prawv", [128, CW], BF16)
        qsw = sb1("qsw", [128, CW], BF16)
        et = [sb1(f"et{i}", [128, CW], BF16) for i in range(4)]
        lnrb = sb1("lnrb", [128, CW], F32)
        invb = [sb1(f"invbb{i}", [128, CW], F32) for i in range(2)]
        vtrash = sb1("vtrash", [128, NT * 128], BF16) if DBG_V_TRASH else None
        pA = ps("pA", [128, 512], F32)
        pB = ps("pB", [128, 512], F32)
        pC = ps("pC", [128, 512], F32)
        pD = ps("pD", [128, 512], F32)
        pE = ps("pE", [128, 512], F32)
        pF = ps("pF", [128, 512], F32)
        pG = ps("pG", [128, 512], F32)
        pH = ps("pH", [128, 512], F32)

        sems = {}
        cnt = {}
        for name in SEM_NAMES:
            sems[name] = ctx.enter_context(nc.semaphore(name))
            cnt[name] = 0
        prog = {k: [] for k in ["sync", "scalar", "vector", "tensor", "gpsimd"]}

        def on(engname):
            def deco(fn):
                prog[engname].append(fn)
                return None
            return deco

        def bump(name, amt=1):
            cnt[name] += amt
            return cnt[name]

        def dma(engname, sem_name, make_out_in, extra_waits=(), ordered=True):
            prev = cnt[sem_name] if ordered else 0
            ws = tuple(extra_waits)

            @on(engname)
            def _(e, sem_name=sem_name, prev=prev, ws=ws, make_out_in=make_out_in):
                for s_, v_ in ws:
                    if v_:
                        e.wait_ge(sems[s_], v_)
                if prev:
                    e.wait_ge(sems[sem_name], prev)
                out, in_, tp = make_out_in()
                if tp:
                    e.dma_start_transpose(out, in_).then_inc(sems[sem_name], 16)
                else:
                    e.dma_start(out=out, in_=in_).then_inc(sems[sem_name], 16)
            return bump(sem_name, 16)

        for dst, src in [(c2, c2_d), (s2, s2_d), (tri, tri_d), (onc, onc_d),
                         (oncf, oncf_d), (onr, onr_d), (onrf, onrf_d), (onm, onm_d)]:
            dma("sync", "c", (lambda dst=dst, src=src: (dst[:, :], src, False)),
                ordered=False)
        consts_c = cnt["c"]
        dma("gpsimd", "bias", (lambda: (biasb[:, :], bias_d, False)))
        bias_v = cnt["bias"]

        @on("vector")
        def _(e):
            e.memset(epsv[:, :], LN_EPS)
            e.memset(bias3[:, :], -3.0)
            e.wait_ge(sems["c"], consts_c)

        bb_banks = [pH, pE, pG, pH]
        bb_pe = [0] * 4
        for j in range(3):
            @on("tensor")
            def _(e, j=j, bv=bias_v, bk=bb_banks[j]):
                if j == 0:
                    e.wait_ge(sems["c"], consts_c)
                    e.wait_ge(sems["bias"], bv)
                e.matmul(bk[:, 0:512], onr[:, :], biasb[0:1, j * 512:(j + 1) * 512],
                         start=True, stop=True).then_inc(sems["pe"], 1)
            bb_pe[j] = bump("pe")
        bb_state = {"biasbc_ac": 0, "pH_gate": 0}

        def plan_bias_copies():
            copy_acs = [0] * 4
            for j in range(3):
                @on("scalar")
                def _(e, j=j, pe_bb=bb_pe[j]):
                    e.wait_ge(sems["pe"], pe_bb)
                    e.activation(out=biasbc[:, j * 512:(j + 1) * 512],
                                 in_=bb_banks[j][:, 0:512],
                                 func=AF.Copy).then_inc(sems["ac"], 1)
                copy_acs[j] = bump("ac")

            @on("tensor")
            def _(e, c0=copy_acs[0]):
                e.wait_ge(sems["ac"], c0)
                e.matmul(pH[:, 0:512], onr[:, :], biasb[0:1, 3 * 512:4 * 512],
                         start=True, stop=True).then_inc(sems["pe"], 1)
            bb_pe[3] = bump("pe")

            @on("scalar")
            def _(e, pe_bb=bb_pe[3]):
                e.wait_ge(sems["pe"], pe_bb)
                e.activation(out=biasbc[:, 3 * 512:4 * 512], in_=pH[:, 0:512],
                             func=AF.Copy).then_inc(sems["ac"], 1)
            copy_acs[3] = bump("ac")
            bb_state["biasbc_ac"] = copy_acs[3]
            bb_state["pH_gate"] = copy_acs[3]

        @on("tensor")
        def _(e):
            e.wait_ge(sems["c"], consts_c)

        xst_free = [("dv", 0), ("dv", 0), ("dv", 0)]
        xnn_free = [("txn", 0), ("txb", 0)]
        ln_state = {"prev_aggr": 0}
        ln_prev = [0, 0, 0]

        def plan_ln_tile(tt):
            b = tt % 3
            b2 = tt % 2
            xs = "x%d" % b
            my_x = dma("gpsimd", xs,
                       (lambda tt=tt, b=b: (xst[b][:, :],
                        bass.AP(x_d.tensor, tt * 128 * E, [[E, 128], [1, E]]), False)),
                       extra_waits=[xst_free[b]])

            @on("vector")
            def _(e, b=b, xs=xs, my_x=my_x, prev_aggr=ln_state["prev_aggr"]):
                e.wait_ge(sems[xs], my_x)
                if prev_aggr:
                    e.wait_ge(sems["dv"], prev_aggr)
                for j in range(3):
                    e.bn_stats(st6[:, 6 * j:6 * j + 6], xst[b][:, 512 * j:512 * (j + 1)])
                e.bn_stats(st6[:, 18:24], xst[b][:, 1536:2048]).then_inc(sems["dv"], 1)
            dv_stats = bump("dv")

            @on("vector")
            def _(e, b=b, dv_stats=dv_stats, prev=ln_prev[b]):
                e.wait_ge(sems["dv"], dv_stats)
                if prev:
                    e.wait_ge(sems["dv"], prev)
                e.bn_aggr(mv[b][:, :], st6[:, :]).then_inc(sems["dv"], 1)
            dv_aggr = bump("dv")
            ln_state["prev_aggr"] = dv_aggr

            @on("scalar")
            def _(e, b=b, dv_aggr=dv_aggr, prev=ln_prev[b]):
                e.wait_ge(sems["dv"], dv_aggr)
                if prev:
                    e.wait_ge(sems["dv"], prev)
                e.activation(out=rstd[b][:, :], in_=mv[b][:, 1:2], func=AF.Ln,
                             bias=epsv[:, :]).then_inc(sems["ac"], 1)
            ac_ln = bump("ac")

            @on("scalar")
            def _(e, b=b, ac_ln=ac_ln):
                e.wait_ge(sems["ac"], ac_ln)
                e.activation(out=rstd[b][:, :], in_=rstd[b][:, :], func=AF.Exp,
                             scale=-0.5).then_inc(sems["ac"], 1)
            ac_rstd = bump("ac")

            @on("vector")
            def _(e, b=b, b2=b2, ac_rstd=ac_rstd, xf=xnn_free[b2]):
                e.wait_ge(sems["ac"], ac_rstd)
                if xf:
                    e.wait_ge(sems[xf[0]], xf[1])
                e.tensor_scalar(out=xnn[b2][:, :], in0=xst[b][:, :], scalar1=mv[b][:, 0:1],
                                scalar2=rstd[b][:, :], op0=ALU.subtract,
                                op1=ALU.mult).then_inc(sems["dv"], 1)
            dv_xn = bump("dv")
            xst_free[b] = ("dv", dv_xn)
            ln_prev[b] = dv_xn

            txs = "txn" if tt % 2 == 0 else "txb"
            xnn_free[b2] = (txs, dma("sync", txs,
                              (lambda tt=tt, b2=b2: (
                                  bass.AP(xnT, tt * 128, [[NT * S, 128], [S, NT], [1, 128]]),
                                  xnn[b2][:, :], True)),
                              extra_waits=[("dv", dv_xn)]))

        plan_ln_tile(0)
        plan_bias_copies()
        for tt in range(1, 4):
            plan_ln_tile(tt)
        xnT_ready = 16 * (NT // 2)

        qa_ac = [0] * NT
        qb_tw = [0] * NT
        qb_sw = [0] * NT
        quantB_state = {"prev_sub": 0}

        def plan_quantA_tile(qt):
            b = qt % 3
            b2 = qt % 2
            xs = "x%d" % b
            my_x = dma("gpsimd", xs,
                       (lambda qt=qt, b=b: (xst[b][:, :],
                        bass.AP(wf_d.tensor, qt * 128 * E, [[E, 128], [1, E]]), False)),
                       extra_waits=[xst_free[b]])

            @on("scalar")
            def _(e, qt=qt, b=b, b2=b2, xs=xs, my_x=my_x):
                e.wait_ge(sems[xs], my_x)
                e.wait_ge(sems["txn"], xnT_ready)
                e.wait_ge(sems["txb"], xnT_ready)
                e.activation(out=xnn[b2][:, :], in_=xst[b][:, :], func=AF.Abs,
                             accum_out=accw[:, qt:qt + 1]).then_inc(sems["ac"], 1)
            qa_ac[qt] = bump("ac")
            xst_free[b] = ("ac", qa_ac[qt])

        nonlocal_holder = {}

        def plan_quant_reduce():
            la = qa_ac[NT - 1]

            @on("vector")
            def _(e, la=la):
                e.wait_ge(sems["ac"], la)
                e.tensor_reduce(out=accr[:, :], in_=accw[:, :], axis=mybir.AxisListType.X,
                                op=ALU.add).then_inc(sems["dv"], 1)
            dv_red = bump("dv")

            @on("tensor")
            def _(e, dv_red=dv_red, acg=cnt["ac"]):
                e.wait_ge(sems["dv"], dv_red)
                e.wait_ge(sems["ac"], acg)
                e.matmul(pF[0:1, 0:1], oncf[:, :], accr[:, :], start=True,
                         stop=True).then_inc(sems["pe"], 1)
            pe_sum = bump("pe")

            @on("vector")
            def _(e, pe_sum=pe_sum):
                e.wait_ge(sems["pe"], pe_sum)
                e.tensor_scalar(out=accr[0:1, 0:1], in0=pF[0:1, 0:1],
                                scalar1=1.0 / float(E * E), scalar2=None,
                                op0=ALU.mult).then_inc(sems["dv"], 1)
            dv_scale = bump("dv")

            @on("tensor")
            def _(e, dv_scale=dv_scale):
                e.wait_ge(sems["dv"], dv_scale)
                e.matmul(pF[:, 0:1], onrf[:, :], accr[0:1, 0:1], start=True,
                         stop=True).then_inc(sems["pe"], 1)
            pe_bc = bump("pe")

            @on("vector")
            def _(e, pe_bc=pe_bc):
                e.wait_ge(sems["pe"], pe_bc)
                e.tensor_copy(sclv[:, :], pF[:, 0:1]).then_inc(sems["dv"], 1)
            d0 = bump("dv")
            nonlocal_holder["rs_pe"] = pe_bc

            @on("vector")
            def _(e, d0=d0):
                e.wait_ge(sems["dv"], d0)
                e.tensor_scalar(out=thrv[:, :], in0=sclv[:, :], scalar1=0.5,
                                scalar2=0.5 * FFN_EPS, op0=ALU.mult,
                                op1=ALU.add).then_inc(sems["dv"], 1)
            d1 = bump("dv")

            @on("vector")
            def _(e, d1=d1):
                e.wait_ge(sems["dv"], d1)
                e.tensor_scalar(out=nthrv[:, :], in0=thrv[:, :], scalar1=-1.0,
                                scalar2=None, op0=ALU.mult).then_inc(sems["dv"], 1)
            return bump("dv")

        def plan_quantB_tile(qt, thr_dv):
            b = qt % 3
            xs = "x%d" % b
            my_x = dma("gpsimd", xs,
                       (lambda qt=qt, b=b: (xst[b][:, :],
                        bass.AP(wf_d.tensor, qt * 128 * E, [[E, 128], [1, E]]), False)),
                       extra_waits=[xst_free[b]])

            wtbf_free = (("tw%d" % ((qt - 1) % 2)), qb_tw[qt - 1]) if qt >= 1 else None
            prev_sub = quantB_state["prev_sub"]
            dv_isge = bump("dv")
            dv_isle = bump("dv")
            dv_sub = bump("dv")
            quantB_state["prev_sub"] = dv_sub

            @on("vector")
            def _(e, qt=qt, b=b, xs=xs, my_x=my_x, thr_dv=thr_dv, wtbf_free=wtbf_free,
                  dv_isge=dv_isge, dv_isle=dv_isle, prev_sub=prev_sub):
                e.wait_ge(sems[xs], my_x)
                if thr_dv:
                    e.wait_ge(sems["dv"], thr_dv)
                if prev_sub:
                    e.wait_ge(sems["dv"], prev_sub)
                e.tensor_scalar(out=xnn[0][:, :], in0=xst[b][:, :], scalar1=thrv[:, :],
                                scalar2=sclv[:, :], op0=ALU.is_ge,
                                op1=ALU.mult).then_inc(sems["dv"], 1)
                e.wait_ge(sems["dv"], dv_isge)
                e.tensor_scalar(out=xnn[1][:, :], in0=xst[b][:, :], scalar1=nthrv[:, :],
                                scalar2=sclv[:, :], op0=ALU.is_le,
                                op1=ALU.mult).then_inc(sems["dv"], 1)
                e.wait_ge(sems["dv"], dv_isle)
                if wtbf_free:
                    e.wait_ge(sems[wtbf_free[0]], wtbf_free[1])
                e.tensor_tensor(out=wtbf[b][:, :], in0=xnn[0][:, :], in1=xnn[1][:, :],
                                op=ALU.subtract).then_inc(sems["dv"], 1)
            xst_free[b] = ("dv", dv_isle)

            tw = "tw%d" % (qt % 2)
            sw = "sw%d" % (qt % 2)
            wtTs_free = (("sw%d" % ((qt - 1) % 2)), qb_sw[qt - 1]) if qt >= 1 else None
            qb_tw[qt] = dma("sync", tw,
                            (lambda b=b: (
                                bass.AP(wtTs[b], 0, [[NT * 128, 128], [128, NT], [1, 128]]),
                                wtbf[b][:, :], True)),
                            extra_waits=[("dv", dv_sub)] +
                                        ([wtTs_free] if wtTs_free else []))
            qb_sw[qt] = dma("sync", sw,
                            (lambda qt=qt, b=b: (
                                bass.AP(wtT_d.tensor, qt * 128, [[E, 128], [128 * E, NT], [1, 128]]),
                                bass.AP(wtTs[b], 0, [[NT * 128, 128], [128, NT], [1, 128]]), False)),
                            extra_waits=[(tw, qb_tw[qt])])

        def plan_head_weights(h, gate_pe):
            out = {}
            hb = h % 2
            for wd, dsts, s_ in ((wq_d, wslq, "wq"), (wk_d, wslk, "wk"), (wv_d, wslv, "wv")):
                if DBG_NO_WSL:
                    out[s_] = cnt[s_]
                    continue
                out[s_] = dma("gpsimd", s_,
                              (lambda wd=wd, dst=dsts[hb], h=h: (dst[:, :],
                               bass.AP(wd.tensor, h * 128, [[E, 128], [E * 128, NT], [1, 128]]),
                               False)),
                              extra_waits=[("pe", gate_pe)])
            return out

        n_heads = H
        do_quant = True
        do_ph2 = True
        if TRUNC == "p0":
            n_heads = 0
            do_quant = False
            do_ph2 = False
        elif TRUNC.startswith("h"):
            n_heads = int(TRUNC[1:])
            do_ph2 = False
        if DBG_NO_QUANT:
            do_quant = False

        wsl_ready = [None] * H
        quant_thr_dv = [0]
        if n_heads:
            wsl_ready[0] = plan_head_weights(0, 0)
        head_proj_done = [0] * H
        oTs_spl = [0, 0]
        qkv_used_pe = [0, 0]
        proj_psum = [pA, pB]
        proj_parity = 0
        proj_evac_ac = [0, 0]
        praw_state = {"k": ("dv", 0), "q": ("dv", 0), "v": ("tv0", 0)}
        pv_banks = [pE, pG]
        pv_gate_dv = [0, 0]
        gchunk = [0]
        rs_gate_ac = [0]
        rs_gate_pe_extra = [0]
        prev_lnrb_ac = [0]
        prev_invb_dv = [0, 0]


        et_slot_pe = [0] * 4
        qsw_free_dv = 0
        prev_norm_dv = 0

        def make_head(h):
            hb = h % 2
            tvh = "tv%d" % hb
            soh = "so%d" % hb
            ch_k = [0] * CH
            ch_q = [0] * CH
            ch_v = [0] * CH

            def plan_proj_chunk(c, h=h, hb=hb, tvh=tvh):
                nonlocal proj_parity, qsw_free_dv
                cs = slice(c * CW, (c + 1) * CW)
                for which, wsls, pr, key, wsem in ((0, wslk, prawk, "k", "wk"),
                                                   (1, wslq, prawq, "q", "wq"),
                                                   (2, wslv, prawv, "v", "wv")):
                    wsl = wsls[hb]
                    pp = proj_psum[proj_parity]
                    free_ac = proj_evac_ac[proj_parity]

                    @on("tensor")
                    def _(e, wsl=wsl, pp=pp, c=c, wsem=wsem, ldw_gate=wsl_ready[h][wsem],
                          free_ac=free_ac,
                          txn_gate=(16 * 2 * (c + 1)) if h == 0 else 0):
                        e.wait_ge(sems[wsem], ldw_gate)
                        if txn_gate:
                            e.wait_ge(sems["txn"], txn_gate)
                            e.wait_ge(sems["txb"], txn_gate)
                        if free_ac:
                            e.wait_ge(sems["dv"], free_ac)
                        for it in range(NT - 1):
                            e.matmul(pp[:, 0:CW], wsl[:, it * 128:(it + 1) * 128],
                                     bass.AP(xnT, it * S + c * CW, [[NT * S, 128], [1, CW]]),
                                     start=(it == 0), stop=False)
                        e.matmul(pp[:, 0:CW], wsl[:, (NT - 1) * 128:NT * 128],
                                 bass.AP(xnT, (NT - 1) * S + c * CW, [[NT * S, 128], [1, CW]]),
                                 start=False, stop=True).then_inc(sems["pe"], 1)
                    pe_proj = bump("pe")

                    psem, pval = praw_state[key]

                    @on("vector")
                    def _(e, pp=pp, pr=pr, pe_proj=pe_proj, psem=psem, pval=pval):
                        e.wait_ge(sems["pe"], pe_proj)
                        if pval:
                            e.wait_ge(sems[psem], pval)
                        e.tensor_copy(pr[:, :], pp[:, 0:CW]).then_inc(sems["dv"], 1)
                    dv_evac = bump("dv")
                    proj_evac_ac[proj_parity] = dv_evac
                    proj_parity ^= 1

                    if which < 2:
                        dstT = kTh[hb] if which == 0 else qTh[hb]
                        dv_swap = bump("dv")
                        dv_mul = bump("dv")
                        dv_rope = bump("dv")

                        @on("vector")
                        def _(e, pr=pr, dstT=dstT, cs=cs, dv_evac=dv_evac,
                              qk_gate=qkv_used_pe[hb] if (h >= 2 and c == 0 and which == 0) else 0,
                              dv_swap=dv_swap, dv_mul=dv_mul, qsw_free=qsw_free_dv):
                            e.wait_ge(sems["dv"], dv_evac)
                            if qk_gate:
                                e.wait_ge(sems["pe"], qk_gate)
                            if qsw_free:
                                e.wait_ge(sems["dv"], qsw_free)
                            e.tensor_copy(qsw[0:64, :], pr[64:128, :])
                            e.tensor_copy(qsw[64:128, :], pr[0:64, :]).then_inc(sems["dv"], 1)
                            e.wait_ge(sems["dv"], dv_swap)
                            e.tensor_tensor(out=dstT[:, cs], in0=pr[:, :], in1=c2[:, cs], op=ALU.mult)
                            e.tensor_tensor(out=qsw[:, :], in0=qsw[:, :], in1=s2[:, cs],
                                            op=ALU.mult).then_inc(sems["dv"], 1)
                            e.wait_ge(sems["dv"], dv_mul)
                            e.tensor_tensor(out=dstT[:, cs], in0=dstT[:, cs], in1=qsw[:, :],
                                            op=ALU.add).then_inc(sems["dv"], 1)
                        praw_state[key] = ("dv", dv_mul)
                        qsw_free_dv = dv_rope
                        if which == 0:
                            ch_k[c] = dv_rope
                        else:
                            ch_q[c] = dv_rope
                    else:
                        ew = [("dv", dv_evac)]
                        if h >= 2 and c == 0:
                            ew.append(("pe", qkv_used_pe[hb]))
                        vdst = vtrash if DBG_V_TRASH else vh[hb]
                        ch_v[c] = dma("sync", tvh,
                                      (lambda pr=pr, vdst=vdst, c=c: (
                                          bass.AP(vdst, 4 * c * 128, [[NT * 128, 128], [128, 4], [1, 128]]),
                                          pr[:, :], True)),
                                      extra_waits=ew)
                        praw_state["v"] = (tvh, ch_v[c])
                if c == CH - 1:
                    head_proj_done[h] = cnt["pe"]

            def plan_att_chunk(c, h=h, hb=hb, tvh=tvh, soh=soh):
                nonlocal prev_norm_dv
                gp = gchunk[0] % 2
                gchunk[0] += 1
                pv = pv_banks[gp]
                cs = slice(c * CW, (c + 1) * CW)
                nkt = 4 * c + 4
                sc_state = {"parity": 0}
                sc_psums = [pC, pD, pH]
                sc_free_ac = [0, 0, 0]
                exp_acs = [0] * nkt
                mask_dvs = [0] * nkt
                kdv, qdv, vr = ch_k[c], ch_q[c], ch_v[c]

                def plan_scores(kt):
                    col0 = max(0, (kt - 4 * c) * 128)
                    n = CW - col0
                    pc_ = sc_psums[sc_state["parity"]]
                    my_free = sc_free_ac[sc_state["parity"]]
                    if my_free == 0 and pc_ is pH and bb_state["pH_gate"]:
                        my_free = bb_state["pH_gate"]
                        bb_state["pH_gate"] = 0

                    @on("tensor")
                    def _(e, pc_=pc_, kt=kt, col0=col0, n=n, my_free=my_free,
                          kdv=kdv, qdv=qdv, cs0=c * CW):
                        e.wait_ge(sems["dv"], max(kdv, qdv))
                        if my_free:
                            e.wait_ge(sems["ac"], my_free)
                        e.matmul(pc_[:, col0:CW], kTh[hb][:, kt * 128:(kt + 1) * 128],
                                 bass.AP(qTh[hb], cs0 + col0, [[S, 128], [1, n]]),
                                 start=True, stop=True).then_inc(sems["pe"], 1)
                    pe_sc = bump("pe")

                    slot = kt % 4
                    et_gate = et_slot_pe[slot]

                    @on("scalar")
                    def _(e, pc_=pc_, col0=col0, slot=slot, pe_sc=pe_sc, et_gate=et_gate):
                        e.wait_ge(sems["pe"], pe_sc)
                        if et_gate:
                            e.wait_ge(sems["pe"], et_gate)
                        e.activation(out=et[slot][:, col0:CW], in_=pc_[:, col0:CW],
                                     func=AF.Exp, scale=INV_SQRT_D,
                                     bias=bias3[:, :]).then_inc(sems["ac"], 1)
                    exp_acs[kt] = bump("ac")
                    sc_free_ac[sc_state["parity"]] = exp_acs[kt]
                    sc_state["parity"] = (sc_state["parity"] + 1) % 3

                    if kt >= 4 * c:
                        @on("vector")
                        def _(e, slot=slot, col0=col0, ac_exp=exp_acs[kt]):
                            e.wait_ge(sems["ac"], ac_exp)
                            e.tensor_tensor(out=et[slot][:, col0:col0 + 128],
                                            in0=et[slot][:, col0:col0 + 128],
                                            in1=tri[:, :], op=ALU.mult).then_inc(sems["dv"], 1)
                        mask_dvs[kt] = bump("dv")

                def plan_rspv(kt):
                    col0 = max(0, (kt - 4 * c) * 128)
                    slot = kt % 4

                    @on("tensor")
                    def _(e, kt=kt, col0=col0, slot=slot, nkt=nkt, tvh=tvh, pv=pv,
                          ac_exp=exp_acs[kt], mdv=mask_dvs[kt],
                          rs_gate=rs_gate_ac[0] if kt == 0 else 0,
                          rs_pe=rs_gate_pe_extra[0] if kt == 0 else 0,
                          pv_gate=pv_gate_dv[gp] if kt == 0 else 0, vr=vr):
                        e.wait_ge(sems["ac"], ac_exp)
                        if mdv:
                            e.wait_ge(sems["dv"], mdv)
                        if kt == 0:
                            if rs_gate:
                                e.wait_ge(sems["ac"], rs_gate)
                            if rs_pe:
                                e.wait_ge(sems["pe"], rs_pe)
                            if pv_gate:
                                e.wait_ge(sems["dv"], pv_gate)
                            e.wait_ge(sems[tvh], vr)
                        e.matmul(pF[:, col0:CW], onm[:, :], et[slot][:, col0:CW],
                                 start=(kt == 0), stop=(kt == nkt - 1))
                        e.matmul(pv[:, col0:CW], vh[hb][:, kt * 128:(kt + 1) * 128],
                                 et[slot][:, col0:CW],
                                 start=(kt == 0), stop=(kt == nkt - 1)).then_inc(sems["pe"], 1)
                    et_slot_pe[slot] = bump("pe")

                for kt in range(nkt):
                    plan_scores(kt)
                    if kt >= 3:
                        plan_rspv(kt - 3)
                for kt in range(max(0, nkt - 3), nkt):
                    plan_rspv(kt)
                pe_pv_done = cnt["pe"]

                @on("scalar")
                def _(e, pe_pv=pe_pv_done, pln=prev_lnrb_ac[0]):
                    e.wait_ge(sems["pe"], pe_pv)
                    if pln:
                        e.wait_ge(sems["ac"], pln)
                    e.activation(out=lnrb[:, :], in_=pF[:, 0:CW], func=AF.Ln).then_inc(sems["ac"], 1)
                ac_lnr = bump("ac")
                rs_gate_ac[0] = ac_lnr

                @on("scalar")
                def _(e, ac_lnr=ac_lnr, gp=gp, pnv=prev_invb_dv[gp]):
                    e.wait_ge(sems["ac"], ac_lnr)
                    if pnv:
                        e.wait_ge(sems["dv"], pnv)
                    e.activation(out=invb[gp][:, :], in_=lnrb[:, :], func=AF.Exp,
                                 scale=-1.0).then_inc(sems["ac"], 1)
                ac_inv = bump("ac")
                prev_lnrb_ac[0] = ac_inv

                def plan_tail(ac_inv=ac_inv, cs=cs, c=c, gp=gp, pv=pv):
                    nonlocal prev_norm_dv
                    dv_norm = bump("dv")

                    @on("vector")
                    def _(e, ac_inv=ac_inv, cs=cs, pv=pv, gp=gp,
                          spl_gate=oTs_spl[hb] if (h >= 2 and c == 0) else 0,
                          prev_norm=prev_norm_dv):
                        e.wait_ge(sems["ac"], ac_inv)
                        if prev_norm:
                            e.wait_ge(sems["dv"], prev_norm)
                        if spl_gate:
                            e.wait_ge(sems[soh], spl_gate)
                        e.tensor_tensor(out=oTs[hb][:, cs], in0=pv[:, 0:CW], in1=invb[gp][:, :],
                                        op=ALU.mult).then_inc(sems["dv"], 1)
                    prev_invb_dv[gp] = dv_norm
                    pv_gate_dv[gp] = dv_norm
                    prev_norm_dv = dv_norm
                return plan_tail

            def plan_background(c, h=h):
                if h == 0 and c < CH - 1:
                    for t2 in range(4 * (c + 1), 4 * (c + 1) + 4):
                        plan_ln_tile(t2)
                if do_quant and n_heads >= 14:
                    if 2 <= h <= 5:
                        plan_quantA_tile(4 * (h - 2) + c)
                    elif h == 6 and c == 0:
                        quant_thr_dv[0] = plan_quant_reduce()
                        rs_gate_pe_extra[0] = nonlocal_holder.get("rs_pe", 0)
                        plan_quantB_tile(0, quant_thr_dv[0])
                    elif 6 <= h <= 13 and c % 2 == 0 and not (h == 6 and c == 0):
                        plan_quantB_tile(2 * (h - 6) + c // 2, 0)

            def finish_head():
                qkv_used_pe[hb] = cnt["pe"]
                if h < H - 2:
                    oTs_spl[hb] = dma("sync", soh,
                                      (lambda h=h, hb=hb: (
                                          bass.AP(outT_d.tensor, h * 128 * S, [[S, 128], [1, S]]),
                                          oTs[hb][:, :], False)),
                                      extra_waits=[("dv", prev_norm_dv)])

            return plan_proj_chunk, plan_att_chunk, plan_background, finish_head

        if n_heads >= 2:
            p0, a0, b0, f0 = make_head(0)
            p1, a1, b1, f1 = make_head(1)
            wsl_ready[1] = plan_head_weights(1, 0)
            p0(0)
            p1(0)
            b0(0)
            p0(1)
            t = a0(0)
            p1(1)
            b0(1)
            p0(2)
            t()
            t = a0(1)
            p1(2)
            b0(2)
            p0(3)
            t()
            t = a0(2)
            p1(3)
            t()
            t = a0(3)
            t()
            t = a1(0)
            t()
            t = a1(1)
            t()
            f0()
            t = a1(2)
            t()
            t = a1(3)
            t()
            f1()
            if n_heads > 2:
                wsl_ready[2] = plan_head_weights(2, head_proj_done[0])
        elif n_heads == 1:
            p0, a0, b0, f0 = make_head(0)
            p0(0)
            b0(0)
            pend_tail = None
            for c in range(1, CH):
                p0(c)
                if pend_tail is not None:
                    pend_tail()
                pend_tail = a0(c - 1)
                b0(c)
            pend_tail()
            pend_tail = a0(CH - 1)
            pend_tail()
            f0()

        for h in range(2, n_heads):
            plan_proj_chunk, plan_att_chunk, plan_background, finish_head = make_head(h)
            plan_proj_chunk(0)
            pend_tail = None
            for c in range(1, CH):
                plan_proj_chunk(c)
                if pend_tail is not None:
                    pend_tail()
                pend_tail = plan_att_chunk(c - 1)
                if c == 1:
                    plan_background(0)
                if c < CH - 1:
                    plan_background(c)
            pend_tail()
            pend_tail = plan_att_chunk(CH - 1)
            pend_tail()
            plan_background(CH - 1)
            finish_head()
            if h + 1 < n_heads:
                wsl_ready[h + 1] = plan_head_weights(h + 1, head_proj_done[h - 1])

        if n_heads == 0:
            for t2 in range(4, NT):
                plan_ln_tile(t2)
        if do_quant and n_heads < 14:
            if n_heads <= 6:
                nA = max(0, (min(n_heads, 6) - 2) * 4)
                for qt in range(nA, NT):
                    plan_quantA_tile(qt)
                thr_dv = plan_quant_reduce()
                for qt in range(NT):
                    plan_quantB_tile(qt, thr_dv if qt < 4 else 0)
            else:
                for qt in range(2 * (n_heads - 6), NT):
                    plan_quantB_tile(qt, 0)

        so_final = [cnt["so0"], cnt["so1"]]
        sw_final = [cnt["sw0"], cnt["sw1"]]
        ph2_finals = [("so0", so_final[0]), ("so1", so_final[1]),
                      ("sw0", sw_final[0]), ("sw1", sw_final[1]),
                      ("tv0", cnt["tv0"]), ("tv1", cnt["tv1"]),
                      ("tw0", cnt["tw0"]), ("tw1", cnt["tw1"]),
                      ("txn", cnt["txn"]), ("txb", cnt["txb"]),
                      ("pe", cnt["pe"]), ("dv", cnt["dv"])]

        with nc.Block("phase01") as block:
            @block.sync
            def _(e):
                for f in prog["sync"]:
                    f(e)

            @block.scalar
            def _(e):
                for f in prog["scalar"]:
                    f(e)

            @block.vector
            def _(e):
                for f in prog["vector"]:
                    f(e)

            @block.tensor
            def _(e):
                for f in prog["tensor"]:
                    f(e)

            @block.gpsimd
            def _(e):
                for f in prog["gpsimd"]:
                    f(e)

        ctx1.close()
        oTh = [ctx.enter_context(nc.sbuf_tensor(f"oTh{i}", [128, H * 1024], BF16))
               for i in range(2)]
        woc = [ctx.enter_context(nc.sbuf_tensor(f"woc{i}", [128, NT * W2], BF16)) for i in range(2)]
        wtc = [ctx.enter_context(nc.sbuf_tensor(f"wtc{i}", [128, NT * W2], BF16)) for i in range(2)]
        xsl = [ctx.enter_context(nc.sbuf_tensor(f"xsl{i}", [128, W2], F32)) for i in range(2)]
        osl = [ctx.enter_context(nc.sbuf_tensor(f"osl{i}", [128, W2], F32)) for i in range(4)]
        prog = {k: [] for k in ["sync", "scalar", "vector", "tensor", "gpsimd"]}

        if DEBUG_DUMP:
            @on("gpsimd")
            def _(e, xr=xnT_ready, bv=cnt["bias"]):
                e.wait_ge(sems["txn"], xr)
                e.wait_ge(sems["txb"], xr)
                e.wait_ge(sems["bias"], bv)
                e.dma_start(out=xnT_d, in_=xnT[:, :]).then_inc(sems["bias"], 16)
            bump("bias", 16)

        ph2_psums = [pA, pB]
        ph2_free_dv = [0, 0]
        woc_used_pe = [0, 0]
        wtc_used_pe = [0, 0]
        xsl_used_dv = [0, 0]
        osl_st = [0, 0, 0, 0]

        chunks = [(half, ec) for half in range(2 if do_ph2 else 0) for ec in range(NEC)]
        wld = {}

        def plan_ph2_weights(idx):
            half, ec = chunks[idx]
            ecb = idx % 2
            wop = "wo%d" % ecb
            wcp = "wc%d" % ecb
            first = idx < 2
            woc_ld = dma("gpsimd", wop,
                         (lambda ec=ec, ecb=ecb: (woc[ecb][:, :],
                          bass.AP(wo_d.tensor, ec * W2, [[E, 128], [E * 128, NT], [1, W2]]),
                          False)),
                         extra_waits=(ph2_finals if first else []) +
                                     [("pe", woc_used_pe[ecb])])
            wtc_ld = dma("scalar", wcp,
                         (lambda ec=ec, ecb=ecb: (wtc[ecb][:, :],
                          bass.AP(wtT_d.tensor, ec * W2, [[E, 128], [128 * E, NT], [1, W2]]),
                          False)),
                         extra_waits=(ph2_finals if first else
                                      [("sw0", sw_final[0]), ("sw1", sw_final[1])]) +
                                     [("pe", wtc_used_pe[ecb])])
            wld[idx] = (woc_ld, wtc_ld)

        if chunks:
            plan_ph2_weights(0)
            plan_ph2_weights(1)
        xsl_pre = {}
        for tb in range(2 if do_ph2 else 0):
            xsl_pre[tb] = dma("sync", "xs%d" % tb,
                              (lambda tb=tb: (xsl[tb][:, :],
                               bass.AP(x_d.tensor, tb * 128 * E, [[E, 128], [1, W2]]),
                               False)),
                              extra_waits=ph2_finals)

        if do_ph2:
            for hh in range(H - 2):
                dma("sync", OTH0[hh % 4],
                    (lambda hh=hh: (
                        bass.AP(oTh[0], hh * 1024, [[H * 1024, 128], [1, 1024]]),
                        bass.AP(outT_d.tensor, hh * 128 * S, [[S, 128], [1, 1024]]),
                        False)),
                    extra_waits=(ph2_finals if hh < 4 else []))
        oTh0_ld = [cnt[s] for s in OTH0]
        oTh1_ld = 0

        def plan_oth1_slice(hh):
            dma("scalar", "oth1",
                (lambda hh=hh: (
                    bass.AP(oTh[1], hh * 1024, [[H * 1024, 128], [1, 1024]]),
                    bass.AP(outT_d.tensor, hh * 128 * S + 1024, [[S, 128], [1, 1024]]),
                    False)),
                extra_waits=(ph2_finals if hh == 0 else []))
        for idx in range(len(chunks)):
            half, ec = chunks[idx]
            ecb = idx % 2
            wop = "wo%d" % ecb
            wcp = "wc%d" % ecb
            if idx + 1 < len(chunks) and idx + 1 not in wld:
                plan_ph2_weights(idx + 1)
            woc_ld, wtc_ld = wld[idx]
            if True:

                for ti in range(8):
                    tt = half * 8 + ti
                    tb = tt % 2
                    xsp = "xs%d" % tb
                    syp = "sy%d" % tb
                    pp = ph2_psums[tb]
                    xsl_first = (half == 0 and ec == 0 and ti < 2)
                    if xsl_first:
                        xsl_ld = xsl_pre[tb]
                    else:
                        xsl_ld = dma("sync", xsp,
                                     (lambda tt=tt, tb=tb, ec=ec: (xsl[tb][:, :],
                                      bass.AP(x_d.tensor, tt * 128 * E + ec * W2, [[E, 128], [1, W2]]),
                                      False)),
                                     extra_waits=[("dv", xsl_used_dv[tb])])

                    @on("tensor")
                    def _(e, pp=pp, ti=ti, tt=tt, ec=ec, ecb=ecb, wop=wop, wcp=wcp,
                          half=half, woc_ld=woc_ld, wtc_ld=wtc_ld,
                          dv_gate=ph2_free_dv[tb], bias_v=bias_v):
                        e.wait_ge(sems[wcp], wtc_ld)
                        e.wait_ge(sems["bias"], bias_v)
                        if dv_gate:
                            e.wait_ge(sems["dv"], dv_gate)
                        for it in range(NT):
                            e.matmul(pp[:, 0:W2],
                                     bass.AP(xnT, it * S + tt * 128, [[NT * S, 128], [1, 128]]),
                                     wtc[ecb][:, it * W2:(it + 1) * W2],
                                     start=(it == 0), stop=False)
                        e.wait_ge(sems[wop], woc_ld)
                        if half == 0:
                            for s_, v_ in zip(OTH0, oTh0_ld):
                                e.wait_ge(sems[s_], v_)
                        else:
                            e.wait_ge(sems["oth1"], oTh1_ld)
                        for it in range(NT):
                            if it >= H - 2:
                                lh = oTs[it % 2][:, tt * 128:(tt + 1) * 128]
                            else:
                                lh = bass.AP(oTh[half], it * 1024 + ti * 128,
                                             [[H * 1024, 128], [1, 128]])
                            mm = e.matmul(pp[:, 0:W2], lh,
                                          woc[ecb][:, it * W2:(it + 1) * W2],
                                          start=False, stop=(it == NT - 1))
                            if it == NT - 1:
                                mm.then_inc(sems["pe"], 1)
                    pe_out = bump("pe")
                    woc_used_pe[ecb] = pe_out
                    wtc_used_pe[ecb] = pe_out


                    dv_out1 = bump("dv")

                    ob = tt % 4

                    @on("vector")
                    def _(e, pp=pp, tb=tb, ob=ob, ec=ec, xsp=xsp, syp=syp, pe_out=pe_out,
                          xsl_ld=xsl_ld, osl_gate=osl_st[ob], dv_out1=dv_out1,
                          bac=bb_state["biasbc_ac"]):
                        e.wait_ge(sems["pe"], pe_out)
                        e.wait_ge(sems[xsp], xsl_ld)
                        e.wait_ge(sems["ac"], bac)
                        if osl_gate:
                            e.wait_ge(sems[syp], osl_gate)
                        e.tensor_tensor(out=osl[ob][:, :], in0=pp[:, 0:W2], in1=xsl[tb][:, :],
                                        op=ALU.add).then_inc(sems["dv"], 1)
                        e.wait_ge(sems["dv"], dv_out1)
                        e.tensor_tensor(out=osl[ob][:, :], in0=osl[ob][:, :],
                                        in1=biasbc[:, ec * W2:(ec + 1) * W2],
                                        op=ALU.add).then_inc(sems["dv"], 1)
                    dv_out = bump("dv")
                    ph2_free_dv[tb] = dv_out
                    xsl_used_dv[tb] = dv_out

                    osl_st[ob] = dma("scalar", syp,
                                     (lambda tt=tt, ob=ob, ec=ec: (
                                         bass.AP(y_d.tensor, tt * 128 * E + ec * W2, [[E, 128], [1, W2]]),
                                         osl[ob][:, :], False)),
                                     extra_waits=[("dv", dv_out)])

            if idx < 7:
                plan_oth1_slice(2 * idx)
                plan_oth1_slice(2 * idx + 1)
        oTh1_ld = cnt["oth1"]

        sy_final = [cnt["sy0"], cnt["sy1"]]

        @on("sync")
        def _(e, sy_final=tuple(sy_final)):
            e.wait_ge(sems["sy0"], sy_final[0])
            e.wait_ge(sems["sy1"], sy_final[1])

        @on("scalar")
        def _(e, sy_final=tuple(sy_final)):
            e.wait_ge(sems["sy0"], sy_final[0])
            e.wait_ge(sems["sy1"], sy_final[1])

        with nc.Block("phase2") as block:
            @block.sync
            def _(e):
                for f in prog["sync"]:
                    f(e)

            @block.scalar
            def _(e):
                for f in prog["scalar"]:
                    f(e)

            @block.vector
            def _(e):
                for f in prog["vector"]:
                    f(e)

            @block.tensor
            def _(e):
                for f in prog["tensor"]:
                    f(e)

            @block.gpsimd
            def _(e):
                for f in prog["gpsimd"]:
                    f(e)

    return nc


def _consts():
    pos = np.arange(S, dtype=np.float64)
    invf = 1.0 / (10000.0 ** (np.arange(0, D, 2, dtype=np.float64) / D))
    ang = invf[:, None] * pos[None, :]
    cos = np.cos(ang).astype(np.float32)
    sin = np.sin(ang).astype(np.float32)
    c2 = np.concatenate([cos, cos], axis=0)
    s2 = np.concatenate([-sin, sin], axis=0)
    tri = np.triu(np.ones((128, 128), np.float32)).astype(BF)
    onc = np.ones((128, 1), BF)
    oncf = np.ones((128, 1), np.float32)
    onr = np.ones((1, 128), BF)
    onrf = np.ones((1, 128), np.float32)
    onm = np.ones((128, 128), BF)
    return c2, s2, tri, onc, oncf, onr, onrf, onm


def kernel(L0_attn_x, L0_attn_wq, L0_attn_wk, L0_attn_wv, L0_attn_wo,
           L0_ffn_weight, L0_ffn_bias):
    x = np.ascontiguousarray(np.asarray(L0_attn_x, dtype=np.float32))
    wq = np.ascontiguousarray(np.asarray(L0_attn_wq, dtype=np.float32))
    wk = np.ascontiguousarray(np.asarray(L0_attn_wk, dtype=np.float32))
    wv = np.ascontiguousarray(np.asarray(L0_attn_wv, dtype=np.float32))
    wo = np.ascontiguousarray(np.asarray(L0_attn_wo, dtype=np.float32))
    wf = np.ascontiguousarray(np.asarray(L0_ffn_weight, dtype=np.float32))
    fb = np.ascontiguousarray(np.asarray(L0_ffn_bias, dtype=np.float32)).reshape(1, E)
    c2, s2, tri, onc, oncf, onr, onrf, onm = _consts()
    nc = build_nc()
    in_maps = []
    for b in range(8):
        in_maps.append({
            "x": x[b], "wq": wq, "wk": wk, "wv": wv, "wo": wo, "wf": wf,
            "fbias": fb, "c2t": c2, "s2t": s2, "tri": tri, "onc": onc,
            "oncf": oncf, "onr": onr, "onrf": onrf, "onm": onm,
        })
    res = run_bass_kernel_spmd(nc, in_maps, core_ids=list(range(8)))
    out = np.stack([np.asarray(r["y"], dtype=np.float32) for r in res.results], axis=0)
    return out


# version 26
# speedup vs baseline: 1.0832x; 1.0042x over previous
import math
import numpy as np
import ml_dtypes
import concourse.bass as bass
import concourse.mybir as mybir
from concourse.bass_utils import run_bass_kernel_spmd
from contextlib import ExitStack

F32 = mybir.dt.float32
BF16 = mybir.dt.bfloat16
AF = mybir.ActivationFunctionType
ALU = mybir.AluOpType
BF = ml_dtypes.bfloat16

E = 2048
S = 2048
H = 16
D = 128
NT = 16
CH = 4
CW = 512
W2 = 256
NEC = E // W2
INV_SQRT_D = 1.0 / math.sqrt(128.0)
LN_EPS = 1e-5
FFN_EPS = 1e-10
DEBUG_DUMP = False
TRUNC = ""
DBG_NO_QUANT = False
DBG_V_TRASH = False
DBG_NO_WSL = False

SEM_NAMES = ["c", "bias", "x0", "x1", "x2", "x3", "wq", "wk", "wv", "txn", "txb", "tv0", "tv1",
             "tw0", "tw1", "so0", "so1", "sw0", "sw1", "oth0", "o0b", "o0c", "o0d",
             "oth1", "wc0", "wc1",
             "wo0", "wo1", "xs0", "xs1", "sy0", "sy1", "dv", "ac", "pe"]
OTH0 = ["oth0", "o0b", "o0c", "o0d"]


def build_nc():
    nc = bass.Bass("TRN2", target_bir_lowering=False, debug=False)
    x_d = nc.dram_tensor("x", [S, E], F32, kind="ExternalInput").ap()
    wq_d = nc.dram_tensor("wq", [E, E], F32, kind="ExternalInput").ap()
    wk_d = nc.dram_tensor("wk", [E, E], F32, kind="ExternalInput").ap()
    wv_d = nc.dram_tensor("wv", [E, E], F32, kind="ExternalInput").ap()
    wo_d = nc.dram_tensor("wo", [E, E], F32, kind="ExternalInput").ap()
    wf_d = nc.dram_tensor("wf", [E, E], F32, kind="ExternalInput").ap()
    bias_d = nc.dram_tensor("fbias", [1, E], F32, kind="ExternalInput").ap()
    c2_d = nc.dram_tensor("c2t", [128, S], F32, kind="ExternalInput").ap()
    s2_d = nc.dram_tensor("s2t", [128, S], F32, kind="ExternalInput").ap()
    tri_d = nc.dram_tensor("tri", [128, 128], BF16, kind="ExternalInput").ap()
    onc_d = nc.dram_tensor("onc", [128, 1], BF16, kind="ExternalInput").ap()
    oncf_d = nc.dram_tensor("oncf", [128, 1], F32, kind="ExternalInput").ap()
    onr_d = nc.dram_tensor("onr", [1, 128], BF16, kind="ExternalInput").ap()
    onm_d = nc.dram_tensor("onm", [128, 128], BF16, kind="ExternalInput").ap()
    onrf_d = nc.dram_tensor("onrf", [1, 128], F32, kind="ExternalInput").ap()
    y_d = nc.dram_tensor("y", [S, E], F32, kind="ExternalOutput").ap()
    scr_kind = "ExternalOutput" if DEBUG_DUMP else "Internal"
    outT_d = nc.dram_tensor("outT_scr", [H, 128, S], BF16, kind=scr_kind).ap()
    wtT_d = nc.dram_tensor("wtT_scr", [NT, 128, E], BF16, kind=scr_kind).ap()
    xnT_d = (nc.dram_tensor("xnT_dump", [128, NT * S], BF16, kind="ExternalOutput").ap()
             if DEBUG_DUMP else None)

    ctx = ExitStack()
    sb = lambda n_, s_, d_: ctx.enter_context(nc.sbuf_tensor(n_, s_, d_))
    ps = lambda n_, s_, d_: ctx.enter_context(nc.psum_tensor(n_, s_, d_))
    with ctx:
        xnT = sb("xnT", [128, NT * S], BF16)
        c2 = sb("c2", [128, S], F32)
        s2 = sb("s2", [128, S], F32)
        tri = sb("trim", [128, 128], BF16)
        onc = sb("oncs", [128, 1], BF16)
        oncf = sb("oncfs", [128, 1], F32)
        onr = sb("onrs", [1, 128], BF16)
        onm = sb("onms", [128, 128], BF16)
        onrf = sb("onrfs", [1, 128], F32)
        biasb = sb("biasb", [1, E], BF16)
        st6 = sb("st6", [128, 24], F32)
        mv = [sb(f"mv{i}", [128, 2], F32) for i in range(3)]
        rstd = [sb(f"rstd{i}", [128, 1], F32) for i in range(3)]
        epsv = sb("epsv", [128, 1], F32)
        bias3 = sb("bias3", [128, 1], F32)
        accw = sb("accw", [128, 16], F32)
        accr = sb("accr", [128, 1], F32)
        sclv = sb("sclv", [128, 1], F32)
        thrv = sb("thrv", [128, 1], F32)
        nthrv = sb("nthrv", [128, 1], F32)
        oTs = [sb(f"oTs{i}", [128, S], BF16) for i in range(2)]
        biasbc = sb("biasbc", [128, E], BF16)
        ctx1 = ExitStack()
        sb1 = lambda n_, s_, d_: ctx1.enter_context(nc.sbuf_tensor(n_, s_, d_))
        xst = [sb1(f"xst{i}", [128, E], F32) for i in range(3)]
        xnn = [sb1(f"xnn{i}", [128, E], BF16) for i in range(2)]
        wtbf = [sb1("wtbf0", [128, E], BF16)] * 4
        wtTs = [sb1("wtTs0", [128, NT * 128], BF16)] * 4
        wslq = [sb1(f"wslq{i}", [128, NT * 128], BF16) for i in range(2)]
        wslk = [sb1(f"wslk{i}", [128, NT * 128], BF16) for i in range(2)]
        wslv = [sb1(f"wslv{i}", [128, NT * 128], BF16) for i in range(2)]
        qTh = [sb1(f"qTh{i}", [128, S], BF16) for i in range(2)]
        kTh = [sb1(f"kTh{i}", [128, S], BF16) for i in range(2)]
        vh = [sb1(f"vh{i}", [128, NT * 128], BF16) for i in range(2)]
        prawk = sb1("prawk", [128, CW], BF16)
        prawq = sb1("prawq", [128, CW], BF16)
        prawv = sb1("prawv", [128, CW], BF16)
        qsw = sb1("qsw", [128, CW], BF16)
        et = [sb1(f"et{i}", [128, CW], BF16) for i in range(4)]
        lnrb = sb1("lnrb", [128, CW], F32)
        invb = [sb1(f"invbb{i}", [128, CW], F32) for i in range(2)]
        vtrash = sb1("vtrash", [128, NT * 128], BF16) if DBG_V_TRASH else None
        pA = ps("pA", [128, 512], F32)
        pB = ps("pB", [128, 512], F32)
        pC = ps("pC", [128, 512], F32)
        pD = ps("pD", [128, 512], F32)
        pE = ps("pE", [128, 512], F32)
        pF = ps("pF", [128, 512], F32)
        pG = ps("pG", [128, 512], F32)
        pH = ps("pH", [128, 512], F32)

        sems = {}
        cnt = {}
        for name in SEM_NAMES:
            sems[name] = ctx.enter_context(nc.semaphore(name))
            cnt[name] = 0
        prog = {k: [] for k in ["sync", "scalar", "vector", "tensor", "gpsimd"]}

        def on(engname):
            def deco(fn):
                prog[engname].append(fn)
                return None
            return deco

        def bump(name, amt=1):
            cnt[name] += amt
            return cnt[name]

        def dma(engname, sem_name, make_out_in, extra_waits=(), ordered=True):
            prev = cnt[sem_name] if ordered else 0
            ws = tuple(extra_waits)

            @on(engname)
            def _(e, sem_name=sem_name, prev=prev, ws=ws, make_out_in=make_out_in):
                for s_, v_ in ws:
                    if v_:
                        e.wait_ge(sems[s_], v_)
                if prev:
                    e.wait_ge(sems[sem_name], prev)
                out, in_, tp = make_out_in()
                if tp:
                    e.dma_start_transpose(out, in_).then_inc(sems[sem_name], 16)
                else:
                    e.dma_start(out=out, in_=in_).then_inc(sems[sem_name], 16)
            return bump(sem_name, 16)

        for dst, src in [(c2, c2_d), (s2, s2_d), (tri, tri_d), (onc, onc_d),
                         (oncf, oncf_d), (onrf, onrf_d), (onm, onm_d)]:
            dma("sync", "c", (lambda dst=dst, src=src: (dst[:, :], src, False)),
                ordered=False)
        consts_c = cnt["c"]
        dma("gpsimd", "bias", (lambda: (biasb[:, :], bias_d, False)),
            ordered=False)
        dma("gpsimd", "bias", (lambda: (onr[:, :], onr_d, False)),
            ordered=False)
        bias_v = cnt["bias"]

        @on("vector")
        def _(e):
            e.memset(epsv[:, :], LN_EPS)
            e.memset(bias3[:, :], -3.0)

        bb_banks = [pH, pE, pG, pH]
        bb_pe = [0] * 4
        for j in range(3):
            @on("tensor")
            def _(e, j=j, bv=bias_v, bk=bb_banks[j]):
                if j == 0:
                    e.wait_ge(sems["bias"], bv)
                e.matmul(bk[:, 0:512], onr[:, :], biasb[0:1, j * 512:(j + 1) * 512],
                         start=True, stop=True).then_inc(sems["pe"], 1)
            bb_pe[j] = bump("pe")
        bb_state = {"biasbc_ac": 0, "pH_gate": 0}

        def plan_bias_copies():
            copy_acs = [0] * 4
            for j in range(3):
                @on("scalar")
                def _(e, j=j, pe_bb=bb_pe[j]):
                    e.wait_ge(sems["pe"], pe_bb)
                    e.activation(out=biasbc[:, j * 512:(j + 1) * 512],
                                 in_=bb_banks[j][:, 0:512],
                                 func=AF.Copy).then_inc(sems["ac"], 1)
                copy_acs[j] = bump("ac")

            @on("tensor")
            def _(e, c0=copy_acs[0]):
                e.wait_ge(sems["ac"], c0)
                e.matmul(pH[:, 0:512], onr[:, :], biasb[0:1, 3 * 512:4 * 512],
                         start=True, stop=True).then_inc(sems["pe"], 1)
            bb_pe[3] = bump("pe")

            @on("scalar")
            def _(e, pe_bb=bb_pe[3]):
                e.wait_ge(sems["pe"], pe_bb)
                e.activation(out=biasbc[:, 3 * 512:4 * 512], in_=pH[:, 0:512],
                             func=AF.Copy).then_inc(sems["ac"], 1)
            copy_acs[3] = bump("ac")
            bb_state["biasbc_ac"] = copy_acs[3]
            bb_state["pH_gate"] = copy_acs[3]

        @on("tensor")
        def _(e):
            e.wait_ge(sems["c"], consts_c)

        xst_free = [("dv", 0), ("dv", 0), ("dv", 0)]
        xnn_free = [("txn", 0), ("txb", 0)]
        ln_state = {"prev_aggr": 0}
        ln_prev = [0, 0, 0]

        def plan_ln_tile(tt):
            b = tt % 3
            b2 = tt % 2
            xs = "x%d" % b
            my_x = dma("gpsimd", xs,
                       (lambda tt=tt, b=b: (xst[b][:, :],
                        bass.AP(x_d.tensor, tt * 128 * E, [[E, 128], [1, E]]), False)),
                       extra_waits=[xst_free[b]])

            @on("vector")
            def _(e, b=b, xs=xs, my_x=my_x, prev_aggr=ln_state["prev_aggr"]):
                e.wait_ge(sems[xs], my_x)
                if prev_aggr:
                    e.wait_ge(sems["dv"], prev_aggr)
                for j in range(3):
                    e.bn_stats(st6[:, 6 * j:6 * j + 6], xst[b][:, 512 * j:512 * (j + 1)])
                e.bn_stats(st6[:, 18:24], xst[b][:, 1536:2048]).then_inc(sems["dv"], 1)
            dv_stats = bump("dv")

            @on("vector")
            def _(e, b=b, dv_stats=dv_stats, prev=ln_prev[b]):
                e.wait_ge(sems["dv"], dv_stats)
                if prev:
                    e.wait_ge(sems["dv"], prev)
                e.bn_aggr(mv[b][:, :], st6[:, :]).then_inc(sems["dv"], 1)
            dv_aggr = bump("dv")
            ln_state["prev_aggr"] = dv_aggr

            @on("scalar")
            def _(e, b=b, dv_aggr=dv_aggr, prev=ln_prev[b]):
                e.wait_ge(sems["dv"], dv_aggr)
                if prev:
                    e.wait_ge(sems["dv"], prev)
                e.activation(out=rstd[b][:, :], in_=mv[b][:, 1:2], func=AF.Ln,
                             bias=epsv[:, :]).then_inc(sems["ac"], 1)
            ac_ln = bump("ac")

            @on("scalar")
            def _(e, b=b, ac_ln=ac_ln):
                e.wait_ge(sems["ac"], ac_ln)
                e.activation(out=rstd[b][:, :], in_=rstd[b][:, :], func=AF.Exp,
                             scale=-0.5).then_inc(sems["ac"], 1)
            ac_rstd = bump("ac")

            @on("vector")
            def _(e, b=b, b2=b2, ac_rstd=ac_rstd, xf=xnn_free[b2]):
                e.wait_ge(sems["ac"], ac_rstd)
                if xf:
                    e.wait_ge(sems[xf[0]], xf[1])
                e.tensor_scalar(out=xnn[b2][:, :], in0=xst[b][:, :], scalar1=mv[b][:, 0:1],
                                scalar2=rstd[b][:, :], op0=ALU.subtract,
                                op1=ALU.mult).then_inc(sems["dv"], 1)
            dv_xn = bump("dv")
            xst_free[b] = ("dv", dv_xn)
            ln_prev[b] = dv_xn

            txs = "txn" if tt % 2 == 0 else "txb"
            xnn_free[b2] = (txs, dma("sync", txs,
                              (lambda tt=tt, b2=b2: (
                                  bass.AP(xnT, tt * 128, [[NT * S, 128], [S, NT], [1, 128]]),
                                  xnn[b2][:, :], True)),
                              extra_waits=[("dv", dv_xn)]))

        plan_ln_tile(0)
        plan_bias_copies()
        for tt in range(1, 4):
            plan_ln_tile(tt)
        xnT_ready = 16 * (NT // 2)

        qa_ac = [0] * NT
        qb_tw = [0] * NT
        qb_sw = [0] * NT
        quantB_state = {"prev_sub": 0}

        def plan_quantA_tile(qt):
            b = qt % 3
            b2 = qt % 2
            xs = "x%d" % b
            my_x = dma("gpsimd", xs,
                       (lambda qt=qt, b=b: (xst[b][:, :],
                        bass.AP(wf_d.tensor, qt * 128 * E, [[E, 128], [1, E]]), False)),
                       extra_waits=[xst_free[b]])

            @on("scalar")
            def _(e, qt=qt, b=b, b2=b2, xs=xs, my_x=my_x):
                e.wait_ge(sems[xs], my_x)
                e.wait_ge(sems["txn"], xnT_ready)
                e.wait_ge(sems["txb"], xnT_ready)
                e.activation(out=xnn[b2][:, :], in_=xst[b][:, :], func=AF.Abs,
                             accum_out=accw[:, qt:qt + 1]).then_inc(sems["ac"], 1)
            qa_ac[qt] = bump("ac")
            xst_free[b] = ("ac", qa_ac[qt])

        nonlocal_holder = {}

        def plan_quant_reduce():
            la = qa_ac[NT - 1]

            @on("vector")
            def _(e, la=la):
                e.wait_ge(sems["ac"], la)
                e.tensor_reduce(out=accr[:, :], in_=accw[:, :], axis=mybir.AxisListType.X,
                                op=ALU.add).then_inc(sems["dv"], 1)
            dv_red = bump("dv")

            @on("tensor")
            def _(e, dv_red=dv_red, acg=cnt["ac"]):
                e.wait_ge(sems["dv"], dv_red)
                e.wait_ge(sems["ac"], acg)
                e.matmul(pF[0:1, 0:1], oncf[:, :], accr[:, :], start=True,
                         stop=True).then_inc(sems["pe"], 1)
            pe_sum = bump("pe")

            @on("vector")
            def _(e, pe_sum=pe_sum):
                e.wait_ge(sems["pe"], pe_sum)
                e.tensor_scalar(out=accr[0:1, 0:1], in0=pF[0:1, 0:1],
                                scalar1=1.0 / float(E * E), scalar2=None,
                                op0=ALU.mult).then_inc(sems["dv"], 1)
            dv_scale = bump("dv")

            @on("tensor")
            def _(e, dv_scale=dv_scale):
                e.wait_ge(sems["dv"], dv_scale)
                e.matmul(pF[:, 0:1], onrf[:, :], accr[0:1, 0:1], start=True,
                         stop=True).then_inc(sems["pe"], 1)
            pe_bc = bump("pe")

            @on("vector")
            def _(e, pe_bc=pe_bc):
                e.wait_ge(sems["pe"], pe_bc)
                e.tensor_copy(sclv[:, :], pF[:, 0:1]).then_inc(sems["dv"], 1)
            d0 = bump("dv")
            nonlocal_holder["rs_pe"] = pe_bc

            @on("vector")
            def _(e, d0=d0):
                e.wait_ge(sems["dv"], d0)
                e.tensor_scalar(out=thrv[:, :], in0=sclv[:, :], scalar1=0.5,
                                scalar2=0.5 * FFN_EPS, op0=ALU.mult,
                                op1=ALU.add).then_inc(sems["dv"], 1)
            d1 = bump("dv")

            @on("vector")
            def _(e, d1=d1):
                e.wait_ge(sems["dv"], d1)
                e.tensor_scalar(out=nthrv[:, :], in0=thrv[:, :], scalar1=-1.0,
                                scalar2=None, op0=ALU.mult).then_inc(sems["dv"], 1)
            return bump("dv")

        def plan_quantB_tile(qt, thr_dv):
            b = qt % 3
            xs = "x%d" % b
            my_x = dma("gpsimd", xs,
                       (lambda qt=qt, b=b: (xst[b][:, :],
                        bass.AP(wf_d.tensor, qt * 128 * E, [[E, 128], [1, E]]), False)),
                       extra_waits=[xst_free[b]])

            wtbf_free = (("tw%d" % ((qt - 1) % 2)), qb_tw[qt - 1]) if qt >= 1 else None
            prev_sub = quantB_state["prev_sub"]
            dv_isge = bump("dv")
            dv_isle = bump("dv")
            dv_sub = bump("dv")
            quantB_state["prev_sub"] = dv_sub

            @on("vector")
            def _(e, qt=qt, b=b, xs=xs, my_x=my_x, thr_dv=thr_dv, wtbf_free=wtbf_free,
                  dv_isge=dv_isge, dv_isle=dv_isle, prev_sub=prev_sub):
                e.wait_ge(sems[xs], my_x)
                if thr_dv:
                    e.wait_ge(sems["dv"], thr_dv)
                if prev_sub:
                    e.wait_ge(sems["dv"], prev_sub)
                e.tensor_scalar(out=xnn[0][:, :], in0=xst[b][:, :], scalar1=thrv[:, :],
                                scalar2=sclv[:, :], op0=ALU.is_ge,
                                op1=ALU.mult).then_inc(sems["dv"], 1)
                e.wait_ge(sems["dv"], dv_isge)
                e.tensor_scalar(out=xnn[1][:, :], in0=xst[b][:, :], scalar1=nthrv[:, :],
                                scalar2=sclv[:, :], op0=ALU.is_le,
                                op1=ALU.mult).then_inc(sems["dv"], 1)
                e.wait_ge(sems["dv"], dv_isle)
                if wtbf_free:
                    e.wait_ge(sems[wtbf_free[0]], wtbf_free[1])
                e.tensor_tensor(out=wtbf[b][:, :], in0=xnn[0][:, :], in1=xnn[1][:, :],
                                op=ALU.subtract).then_inc(sems["dv"], 1)
            xst_free[b] = ("dv", dv_isle)

            tw = "tw%d" % (qt % 2)
            sw = "sw%d" % (qt % 2)
            wtTs_free = (("sw%d" % ((qt - 1) % 2)), qb_sw[qt - 1]) if qt >= 1 else None
            qb_tw[qt] = dma("sync", tw,
                            (lambda b=b: (
                                bass.AP(wtTs[b], 0, [[NT * 128, 128], [128, NT], [1, 128]]),
                                wtbf[b][:, :], True)),
                            extra_waits=[("dv", dv_sub)] +
                                        ([wtTs_free] if wtTs_free else []))
            qb_sw[qt] = dma("sync", sw,
                            (lambda qt=qt, b=b: (
                                bass.AP(wtT_d.tensor, qt * 128, [[E, 128], [128 * E, NT], [1, 128]]),
                                bass.AP(wtTs[b], 0, [[NT * 128, 128], [128, NT], [1, 128]]), False)),
                            extra_waits=[(tw, qb_tw[qt])])

        def plan_head_weights(h, gate_pe):
            out = {}
            hb = h % 2
            for wd, dsts, s_ in ((wq_d, wslq, "wq"), (wk_d, wslk, "wk"), (wv_d, wslv, "wv")):
                if DBG_NO_WSL:
                    out[s_] = cnt[s_]
                    continue
                out[s_] = dma("gpsimd", s_,
                              (lambda wd=wd, dst=dsts[hb], h=h: (dst[:, :],
                               bass.AP(wd.tensor, h * 128, [[E, 128], [E * 128, NT], [1, 128]]),
                               False)),
                              extra_waits=[("pe", gate_pe)])
            return out

        n_heads = H
        do_quant = True
        do_ph2 = True
        if TRUNC == "p0":
            n_heads = 0
            do_quant = False
            do_ph2 = False
        elif TRUNC.startswith("h"):
            n_heads = int(TRUNC[1:])
            do_ph2 = False
        if DBG_NO_QUANT:
            do_quant = False

        wsl_ready = [None] * H
        quant_thr_dv = [0]
        if n_heads:
            wsl_ready[0] = plan_head_weights(0, 0)
        head_proj_done = [0] * H
        oTs_spl = [0, 0]
        qkv_used_pe = [0, 0]
        proj_psum = [pA, pB]
        proj_parity = 0
        proj_evac_ac = [0, 0]
        praw_state = {"k": ("dv", 0), "q": ("dv", 0), "v": ("tv0", 0)}
        pv_banks = [pE, pG]
        pv_gate_dv = [0, 0]
        gchunk = [0]
        rs_gate_ac = [0]
        rs_gate_pe_extra = [0]
        prev_lnrb_ac = [0]
        prev_invb_dv = [0, 0]


        et_slot_pe = [0] * 4
        qsw_free_dv = 0
        prev_norm_dv = 0

        def make_head(h):
            hb = h % 2
            tvh = "tv%d" % hb
            soh = "so%d" % hb
            ch_k = [0] * CH
            ch_q = [0] * CH
            ch_v = [0] * CH

            def plan_proj_chunk(c, h=h, hb=hb, tvh=tvh):
                nonlocal proj_parity, qsw_free_dv
                cs = slice(c * CW, (c + 1) * CW)
                for which, wsls, pr, key, wsem in ((0, wslk, prawk, "k", "wk"),
                                                   (1, wslq, prawq, "q", "wq"),
                                                   (2, wslv, prawv, "v", "wv")):
                    wsl = wsls[hb]
                    pp = proj_psum[proj_parity]
                    free_ac = proj_evac_ac[proj_parity]

                    @on("tensor")
                    def _(e, wsl=wsl, pp=pp, c=c, wsem=wsem, ldw_gate=wsl_ready[h][wsem],
                          free_ac=free_ac,
                          txn_gate=(16 * 2 * (c + 1)) if h == 0 else 0):
                        e.wait_ge(sems[wsem], ldw_gate)
                        if txn_gate:
                            e.wait_ge(sems["txn"], txn_gate)
                            e.wait_ge(sems["txb"], txn_gate)
                        if free_ac:
                            e.wait_ge(sems["dv"], free_ac)
                        for it in range(NT - 1):
                            e.matmul(pp[:, 0:CW], wsl[:, it * 128:(it + 1) * 128],
                                     bass.AP(xnT, it * S + c * CW, [[NT * S, 128], [1, CW]]),
                                     start=(it == 0), stop=False)
                        e.matmul(pp[:, 0:CW], wsl[:, (NT - 1) * 128:NT * 128],
                                 bass.AP(xnT, (NT - 1) * S + c * CW, [[NT * S, 128], [1, CW]]),
                                 start=False, stop=True).then_inc(sems["pe"], 1)
                    pe_proj = bump("pe")

                    psem, pval = praw_state[key]

                    @on("vector")
                    def _(e, pp=pp, pr=pr, pe_proj=pe_proj, psem=psem, pval=pval):
                        e.wait_ge(sems["pe"], pe_proj)
                        if pval:
                            e.wait_ge(sems[psem], pval)
                        e.tensor_copy(pr[:, :], pp[:, 0:CW]).then_inc(sems["dv"], 1)
                    dv_evac = bump("dv")
                    proj_evac_ac[proj_parity] = dv_evac
                    proj_parity ^= 1

                    if which < 2:
                        dstT = kTh[hb] if which == 0 else qTh[hb]
                        dv_swap = bump("dv")
                        dv_mul = bump("dv")
                        dv_rope = bump("dv")

                        @on("vector")
                        def _(e, pr=pr, dstT=dstT, cs=cs, dv_evac=dv_evac,
                              qk_gate=qkv_used_pe[hb] if (h >= 2 and c == 0 and which == 0) else 0,
                              dv_swap=dv_swap, dv_mul=dv_mul, qsw_free=qsw_free_dv):
                            e.wait_ge(sems["dv"], dv_evac)
                            if qk_gate:
                                e.wait_ge(sems["pe"], qk_gate)
                            if qsw_free:
                                e.wait_ge(sems["dv"], qsw_free)
                            e.tensor_copy(qsw[0:64, :], pr[64:128, :])
                            e.tensor_copy(qsw[64:128, :], pr[0:64, :]).then_inc(sems["dv"], 1)
                            e.wait_ge(sems["dv"], dv_swap)
                            e.tensor_tensor(out=dstT[:, cs], in0=pr[:, :], in1=c2[:, cs], op=ALU.mult)
                            e.tensor_tensor(out=qsw[:, :], in0=qsw[:, :], in1=s2[:, cs],
                                            op=ALU.mult).then_inc(sems["dv"], 1)
                            e.wait_ge(sems["dv"], dv_mul)
                            e.tensor_tensor(out=dstT[:, cs], in0=dstT[:, cs], in1=qsw[:, :],
                                            op=ALU.add).then_inc(sems["dv"], 1)
                        praw_state[key] = ("dv", dv_mul)
                        qsw_free_dv = dv_rope
                        if which == 0:
                            ch_k[c] = dv_rope
                        else:
                            ch_q[c] = dv_rope
                    else:
                        ew = [("dv", dv_evac)]
                        if h >= 2 and c == 0:
                            ew.append(("pe", qkv_used_pe[hb]))
                        vdst = vtrash if DBG_V_TRASH else vh[hb]
                        ch_v[c] = dma("sync", tvh,
                                      (lambda pr=pr, vdst=vdst, c=c: (
                                          bass.AP(vdst, 4 * c * 128, [[NT * 128, 128], [128, 4], [1, 128]]),
                                          pr[:, :], True)),
                                      extra_waits=ew)
                        praw_state["v"] = (tvh, ch_v[c])
                if c == CH - 1:
                    head_proj_done[h] = cnt["pe"]

            def plan_att_chunk(c, h=h, hb=hb, tvh=tvh, soh=soh):
                nonlocal prev_norm_dv
                gp = gchunk[0] % 2
                gchunk[0] += 1
                pv = pv_banks[gp]
                cs = slice(c * CW, (c + 1) * CW)
                nkt = 4 * c + 4
                sc_state = {"parity": 0}
                sc_psums = [pC, pD, pH]
                sc_free_ac = [0, 0, 0]
                exp_acs = [0] * nkt
                mask_dvs = [0] * nkt
                kdv, qdv, vr = ch_k[c], ch_q[c], ch_v[c]

                def plan_scores(kt):
                    col0 = max(0, (kt - 4 * c) * 128)
                    n = CW - col0
                    pc_ = sc_psums[sc_state["parity"]]
                    my_free = sc_free_ac[sc_state["parity"]]
                    if my_free == 0 and pc_ is pH and bb_state["pH_gate"]:
                        my_free = bb_state["pH_gate"]
                        bb_state["pH_gate"] = 0

                    @on("tensor")
                    def _(e, pc_=pc_, kt=kt, col0=col0, n=n, my_free=my_free,
                          kdv=kdv, qdv=qdv, cs0=c * CW):
                        e.wait_ge(sems["dv"], max(kdv, qdv))
                        if my_free:
                            e.wait_ge(sems["ac"], my_free)
                        e.matmul(pc_[:, col0:CW], kTh[hb][:, kt * 128:(kt + 1) * 128],
                                 bass.AP(qTh[hb], cs0 + col0, [[S, 128], [1, n]]),
                                 start=True, stop=True).then_inc(sems["pe"], 1)
                    pe_sc = bump("pe")

                    slot = kt % 4
                    et_gate = et_slot_pe[slot]

                    @on("scalar")
                    def _(e, pc_=pc_, col0=col0, slot=slot, pe_sc=pe_sc, et_gate=et_gate):
                        e.wait_ge(sems["pe"], pe_sc)
                        if et_gate:
                            e.wait_ge(sems["pe"], et_gate)
                        e.activation(out=et[slot][:, col0:CW], in_=pc_[:, col0:CW],
                                     func=AF.Exp, scale=INV_SQRT_D,
                                     bias=bias3[:, :]).then_inc(sems["ac"], 1)
                    exp_acs[kt] = bump("ac")
                    sc_free_ac[sc_state["parity"]] = exp_acs[kt]
                    sc_state["parity"] = (sc_state["parity"] + 1) % 3

                    if kt >= 4 * c:
                        @on("vector")
                        def _(e, slot=slot, col0=col0, ac_exp=exp_acs[kt]):
                            e.wait_ge(sems["ac"], ac_exp)
                            e.tensor_tensor(out=et[slot][:, col0:col0 + 128],
                                            in0=et[slot][:, col0:col0 + 128],
                                            in1=tri[:, :], op=ALU.mult).then_inc(sems["dv"], 1)
                        mask_dvs[kt] = bump("dv")

                def plan_rspv(kt):
                    col0 = max(0, (kt - 4 * c) * 128)
                    slot = kt % 4

                    @on("tensor")
                    def _(e, kt=kt, col0=col0, slot=slot, nkt=nkt, tvh=tvh, pv=pv,
                          ac_exp=exp_acs[kt], mdv=mask_dvs[kt],
                          rs_gate=rs_gate_ac[0] if kt == 0 else 0,
                          rs_pe=rs_gate_pe_extra[0] if kt == 0 else 0,
                          pv_gate=pv_gate_dv[gp] if kt == 0 else 0, vr=vr):
                        e.wait_ge(sems["ac"], ac_exp)
                        if mdv:
                            e.wait_ge(sems["dv"], mdv)
                        if kt == 0:
                            if rs_gate:
                                e.wait_ge(sems["ac"], rs_gate)
                            if rs_pe:
                                e.wait_ge(sems["pe"], rs_pe)
                            if pv_gate:
                                e.wait_ge(sems["dv"], pv_gate)
                            e.wait_ge(sems[tvh], vr)
                        e.matmul(pF[:, col0:CW], onm[:, :], et[slot][:, col0:CW],
                                 start=(kt == 0), stop=(kt == nkt - 1))
                        e.matmul(pv[:, col0:CW], vh[hb][:, kt * 128:(kt + 1) * 128],
                                 et[slot][:, col0:CW],
                                 start=(kt == 0), stop=(kt == nkt - 1)).then_inc(sems["pe"], 1)
                    et_slot_pe[slot] = bump("pe")

                for kt in range(nkt):
                    plan_scores(kt)
                    if kt >= 3:
                        plan_rspv(kt - 3)
                for kt in range(max(0, nkt - 3), nkt):
                    plan_rspv(kt)
                pe_pv_done = cnt["pe"]

                @on("scalar")
                def _(e, pe_pv=pe_pv_done, pln=prev_lnrb_ac[0]):
                    e.wait_ge(sems["pe"], pe_pv)
                    if pln:
                        e.wait_ge(sems["ac"], pln)
                    e.activation(out=lnrb[:, :], in_=pF[:, 0:CW], func=AF.Ln).then_inc(sems["ac"], 1)
                ac_lnr = bump("ac")
                rs_gate_ac[0] = ac_lnr

                @on("scalar")
                def _(e, ac_lnr=ac_lnr, gp=gp, pnv=prev_invb_dv[gp]):
                    e.wait_ge(sems["ac"], ac_lnr)
                    if pnv:
                        e.wait_ge(sems["dv"], pnv)
                    e.activation(out=invb[gp][:, :], in_=lnrb[:, :], func=AF.Exp,
                                 scale=-1.0).then_inc(sems["ac"], 1)
                ac_inv = bump("ac")
                prev_lnrb_ac[0] = ac_inv

                def plan_tail(ac_inv=ac_inv, cs=cs, c=c, gp=gp, pv=pv):
                    nonlocal prev_norm_dv
                    dv_norm = bump("dv")

                    @on("vector")
                    def _(e, ac_inv=ac_inv, cs=cs, pv=pv, gp=gp,
                          spl_gate=oTs_spl[hb] if (h >= 2 and c == 0) else 0,
                          prev_norm=prev_norm_dv):
                        e.wait_ge(sems["ac"], ac_inv)
                        if prev_norm:
                            e.wait_ge(sems["dv"], prev_norm)
                        if spl_gate:
                            e.wait_ge(sems[soh], spl_gate)
                        e.tensor_tensor(out=oTs[hb][:, cs], in0=pv[:, 0:CW], in1=invb[gp][:, :],
                                        op=ALU.mult).then_inc(sems["dv"], 1)
                    prev_invb_dv[gp] = dv_norm
                    pv_gate_dv[gp] = dv_norm
                    prev_norm_dv = dv_norm
                return plan_tail

            def plan_background(c, h=h):
                if h == 0 and c < CH - 1:
                    for t2 in range(4 * (c + 1), 4 * (c + 1) + 4):
                        plan_ln_tile(t2)
                if do_quant and n_heads >= 14:
                    if 2 <= h <= 5:
                        plan_quantA_tile(4 * (h - 2) + c)
                    elif h == 6 and c == 0:
                        quant_thr_dv[0] = plan_quant_reduce()
                        rs_gate_pe_extra[0] = nonlocal_holder.get("rs_pe", 0)
                        plan_quantB_tile(0, quant_thr_dv[0])
                    elif 6 <= h <= 13 and c % 2 == 0 and not (h == 6 and c == 0):
                        plan_quantB_tile(2 * (h - 6) + c // 2, 0)

            def finish_head():
                qkv_used_pe[hb] = cnt["pe"]
                if h < H - 2:
                    oTs_spl[hb] = dma("sync", soh,
                                      (lambda h=h, hb=hb: (
                                          bass.AP(outT_d.tensor, h * 128 * S, [[S, 128], [1, S]]),
                                          oTs[hb][:, :], False)),
                                      extra_waits=[("dv", prev_norm_dv)])

            return plan_proj_chunk, plan_att_chunk, plan_background, finish_head

        @on("vector")
        def _(e, cc=consts_c):
            e.wait_ge(sems["c"], cc)

        if n_heads >= 2:
            p0, a0, b0, f0 = make_head(0)
            p1, a1, b1, f1 = make_head(1)
            wsl_ready[1] = plan_head_weights(1, 0)
            p0(0)
            p1(0)
            b0(0)
            t0 = a0(0)
            p0(1)
            t1 = a1(0)
            t0()
            p1(1)
            b0(1)
            t0 = a0(1)
            p0(2)
            t1()
            t1 = a1(1)
            t0()
            p1(2)
            b0(2)
            t0 = a0(2)
            p0(3)
            t1()
            t1 = a1(2)
            t0()
            p1(3)
            t0 = a0(3)
            t1()
            t1 = a1(3)
            t0()
            f0()
            t1()
            f1()
            if n_heads > 2:
                wsl_ready[2] = plan_head_weights(2, head_proj_done[0])
        elif n_heads == 1:
            p0, a0, b0, f0 = make_head(0)
            p0(0)
            b0(0)
            pend_tail = None
            for c in range(1, CH):
                p0(c)
                if pend_tail is not None:
                    pend_tail()
                pend_tail = a0(c - 1)
                b0(c)
            pend_tail()
            pend_tail = a0(CH - 1)
            pend_tail()
            f0()

        for h in range(2, n_heads):
            plan_proj_chunk, plan_att_chunk, plan_background, finish_head = make_head(h)
            plan_proj_chunk(0)
            pend_tail = None
            for c in range(1, CH):
                plan_proj_chunk(c)
                if pend_tail is not None:
                    pend_tail()
                pend_tail = plan_att_chunk(c - 1)
                if c == 1:
                    plan_background(0)
                if c < CH - 1:
                    plan_background(c)
            pend_tail()
            pend_tail = plan_att_chunk(CH - 1)
            pend_tail()
            plan_background(CH - 1)
            finish_head()
            if h + 1 < n_heads:
                wsl_ready[h + 1] = plan_head_weights(h + 1, head_proj_done[h - 1])

        if n_heads == 0:
            for t2 in range(4, NT):
                plan_ln_tile(t2)
        if do_quant and n_heads < 14:
            if n_heads <= 6:
                nA = max(0, (min(n_heads, 6) - 2) * 4)
                for qt in range(nA, NT):
                    plan_quantA_tile(qt)
                thr_dv = plan_quant_reduce()
                for qt in range(NT):
                    plan_quantB_tile(qt, thr_dv if qt < 4 else 0)
            else:
                for qt in range(2 * (n_heads - 6), NT):
                    plan_quantB_tile(qt, 0)

        so_final = [cnt["so0"], cnt["so1"]]
        sw_final = [cnt["sw0"], cnt["sw1"]]
        ph2_finals = [("so0", so_final[0]), ("so1", so_final[1]),
                      ("sw0", sw_final[0]), ("sw1", sw_final[1]),
                      ("tv0", cnt["tv0"]), ("tv1", cnt["tv1"]),
                      ("tw0", cnt["tw0"]), ("tw1", cnt["tw1"]),
                      ("txn", cnt["txn"]), ("txb", cnt["txb"]),
                      ("pe", cnt["pe"]), ("dv", cnt["dv"])]

        with nc.Block("phase01") as block:
            @block.sync
            def _(e):
                for f in prog["sync"]:
                    f(e)

            @block.scalar
            def _(e):
                for f in prog["scalar"]:
                    f(e)

            @block.vector
            def _(e):
                for f in prog["vector"]:
                    f(e)

            @block.tensor
            def _(e):
                for f in prog["tensor"]:
                    f(e)

            @block.gpsimd
            def _(e):
                for f in prog["gpsimd"]:
                    f(e)

        ctx1.close()
        oTh = [ctx.enter_context(nc.sbuf_tensor(f"oTh{i}", [128, H * 1024], BF16))
               for i in range(2)]
        woc = [ctx.enter_context(nc.sbuf_tensor(f"woc{i}", [128, NT * W2], BF16)) for i in range(2)]
        wtc = [ctx.enter_context(nc.sbuf_tensor(f"wtc{i}", [128, NT * W2], BF16)) for i in range(2)]
        xsl = [ctx.enter_context(nc.sbuf_tensor(f"xsl{i}", [128, W2], F32)) for i in range(2)]
        osl = [ctx.enter_context(nc.sbuf_tensor(f"osl{i}", [128, W2], F32)) for i in range(4)]
        prog = {k: [] for k in ["sync", "scalar", "vector", "tensor", "gpsimd"]}

        if DEBUG_DUMP:
            @on("gpsimd")
            def _(e, xr=xnT_ready, bv=cnt["bias"]):
                e.wait_ge(sems["txn"], xr)
                e.wait_ge(sems["txb"], xr)
                e.wait_ge(sems["bias"], bv)
                e.dma_start(out=xnT_d, in_=xnT[:, :]).then_inc(sems["bias"], 16)
            bump("bias", 16)

        ph2_psums = [pA, pB]
        ph2_free_dv = [0, 0]
        woc_used_pe = [0, 0]
        wtc_used_pe = [0, 0]
        xsl_used_dv = [0, 0]
        osl_st = [0, 0, 0, 0]

        chunks = [(half, ec) for half in range(2 if do_ph2 else 0) for ec in range(NEC)]
        wld = {}

        def plan_ph2_weights(idx):
            half, ec = chunks[idx]
            ecb = idx % 2
            wop = "wo%d" % ecb
            wcp = "wc%d" % ecb
            first = idx < 2
            woc_ld = dma("gpsimd", wop,
                         (lambda ec=ec, ecb=ecb: (woc[ecb][:, :],
                          bass.AP(wo_d.tensor, ec * W2, [[E, 128], [E * 128, NT], [1, W2]]),
                          False)),
                         extra_waits=(ph2_finals if first else []) +
                                     [("pe", woc_used_pe[ecb])])
            wtc_ld = dma("scalar", wcp,
                         (lambda ec=ec, ecb=ecb: (wtc[ecb][:, :],
                          bass.AP(wtT_d.tensor, ec * W2, [[E, 128], [128 * E, NT], [1, W2]]),
                          False)),
                         extra_waits=(ph2_finals if first else
                                      [("sw0", sw_final[0]), ("sw1", sw_final[1])]) +
                                     [("pe", wtc_used_pe[ecb])])
            wld[idx] = (woc_ld, wtc_ld)

        if chunks:
            plan_ph2_weights(0)
            plan_ph2_weights(1)
        xsl_pre = {}
        for tb in range(2 if do_ph2 else 0):
            xsl_pre[tb] = dma("sync", "xs%d" % tb,
                              (lambda tb=tb: (xsl[tb][:, :],
                               bass.AP(x_d.tensor, tb * 128 * E, [[E, 128], [1, W2]]),
                               False)),
                              extra_waits=ph2_finals)

        if do_ph2:
            for hh in range(H - 2):
                dma("sync", OTH0[hh % 4],
                    (lambda hh=hh: (
                        bass.AP(oTh[0], hh * 1024, [[H * 1024, 128], [1, 1024]]),
                        bass.AP(outT_d.tensor, hh * 128 * S, [[S, 128], [1, 1024]]),
                        False)),
                    extra_waits=(ph2_finals if hh < 4 else []))
        oTh0_ld = [cnt[s] for s in OTH0]
        oTh1_ld = 0

        def plan_oth1_slice(hh):
            dma("scalar", "oth1",
                (lambda hh=hh: (
                    bass.AP(oTh[1], hh * 1024, [[H * 1024, 128], [1, 1024]]),
                    bass.AP(outT_d.tensor, hh * 128 * S + 1024, [[S, 128], [1, 1024]]),
                    False)),
                extra_waits=(ph2_finals if hh == 0 else []))
        for idx in range(len(chunks)):
            half, ec = chunks[idx]
            ecb = idx % 2
            wop = "wo%d" % ecb
            wcp = "wc%d" % ecb
            if idx + 1 < len(chunks) and idx + 1 not in wld:
                plan_ph2_weights(idx + 1)
            woc_ld, wtc_ld = wld[idx]
            if True:

                for ti in range(8):
                    tt = half * 8 + ti
                    tb = tt % 2
                    xsp = "xs%d" % tb
                    syp = "sy%d" % tb
                    pp = ph2_psums[tb]
                    xsl_first = (half == 0 and ec == 0 and ti < 2)
                    if xsl_first:
                        xsl_ld = xsl_pre[tb]
                    else:
                        xsl_ld = dma("sync", xsp,
                                     (lambda tt=tt, tb=tb, ec=ec: (xsl[tb][:, :],
                                      bass.AP(x_d.tensor, tt * 128 * E + ec * W2, [[E, 128], [1, W2]]),
                                      False)),
                                     extra_waits=[("dv", xsl_used_dv[tb])])

                    @on("tensor")
                    def _(e, pp=pp, ti=ti, tt=tt, ec=ec, ecb=ecb, wop=wop, wcp=wcp,
                          half=half, woc_ld=woc_ld, wtc_ld=wtc_ld,
                          dv_gate=ph2_free_dv[tb], bias_v=bias_v):
                        e.wait_ge(sems[wcp], wtc_ld)
                        e.wait_ge(sems["bias"], bias_v)
                        if dv_gate:
                            e.wait_ge(sems["dv"], dv_gate)
                        for it in range(NT):
                            e.matmul(pp[:, 0:W2],
                                     bass.AP(xnT, it * S + tt * 128, [[NT * S, 128], [1, 128]]),
                                     wtc[ecb][:, it * W2:(it + 1) * W2],
                                     start=(it == 0), stop=False)
                        e.wait_ge(sems[wop], woc_ld)
                        if half == 0:
                            for s_, v_ in zip(OTH0, oTh0_ld):
                                e.wait_ge(sems[s_], v_)
                        else:
                            e.wait_ge(sems["oth1"], oTh1_ld)
                        for it in range(NT):
                            if it >= H - 2:
                                lh = oTs[it % 2][:, tt * 128:(tt + 1) * 128]
                            else:
                                lh = bass.AP(oTh[half], it * 1024 + ti * 128,
                                             [[H * 1024, 128], [1, 128]])
                            mm = e.matmul(pp[:, 0:W2], lh,
                                          woc[ecb][:, it * W2:(it + 1) * W2],
                                          start=False, stop=(it == NT - 1))
                            if it == NT - 1:
                                mm.then_inc(sems["pe"], 1)
                    pe_out = bump("pe")
                    woc_used_pe[ecb] = pe_out
                    wtc_used_pe[ecb] = pe_out


                    dv_out1 = bump("dv")

                    ob = tt % 4

                    @on("vector")
                    def _(e, pp=pp, tb=tb, ob=ob, ec=ec, xsp=xsp, syp=syp, pe_out=pe_out,
                          xsl_ld=xsl_ld, osl_gate=osl_st[ob], dv_out1=dv_out1,
                          bac=bb_state["biasbc_ac"]):
                        e.wait_ge(sems["pe"], pe_out)
                        e.wait_ge(sems[xsp], xsl_ld)
                        e.wait_ge(sems["ac"], bac)
                        if osl_gate:
                            e.wait_ge(sems[syp], osl_gate)
                        e.tensor_tensor(out=osl[ob][:, :], in0=pp[:, 0:W2], in1=xsl[tb][:, :],
                                        op=ALU.add).then_inc(sems["dv"], 1)
                        e.wait_ge(sems["dv"], dv_out1)
                        e.tensor_tensor(out=osl[ob][:, :], in0=osl[ob][:, :],
                                        in1=biasbc[:, ec * W2:(ec + 1) * W2],
                                        op=ALU.add).then_inc(sems["dv"], 1)
                    dv_out = bump("dv")
                    ph2_free_dv[tb] = dv_out
                    xsl_used_dv[tb] = dv_out

                    osl_st[ob] = dma("scalar", syp,
                                     (lambda tt=tt, ob=ob, ec=ec: (
                                         bass.AP(y_d.tensor, tt * 128 * E + ec * W2, [[E, 128], [1, W2]]),
                                         osl[ob][:, :], False)),
                                     extra_waits=[("dv", dv_out)])

            if idx < 7:
                plan_oth1_slice(2 * idx)
                plan_oth1_slice(2 * idx + 1)
        oTh1_ld = cnt["oth1"]

        sy_final = [cnt["sy0"], cnt["sy1"]]

        @on("sync")
        def _(e, sy_final=tuple(sy_final)):
            e.wait_ge(sems["sy0"], sy_final[0])
            e.wait_ge(sems["sy1"], sy_final[1])

        @on("scalar")
        def _(e, sy_final=tuple(sy_final)):
            e.wait_ge(sems["sy0"], sy_final[0])
            e.wait_ge(sems["sy1"], sy_final[1])

        with nc.Block("phase2") as block:
            @block.sync
            def _(e):
                for f in prog["sync"]:
                    f(e)

            @block.scalar
            def _(e):
                for f in prog["scalar"]:
                    f(e)

            @block.vector
            def _(e):
                for f in prog["vector"]:
                    f(e)

            @block.tensor
            def _(e):
                for f in prog["tensor"]:
                    f(e)

            @block.gpsimd
            def _(e):
                for f in prog["gpsimd"]:
                    f(e)

    return nc


def _consts():
    pos = np.arange(S, dtype=np.float64)
    invf = 1.0 / (10000.0 ** (np.arange(0, D, 2, dtype=np.float64) / D))
    ang = invf[:, None] * pos[None, :]
    cos = np.cos(ang).astype(np.float32)
    sin = np.sin(ang).astype(np.float32)
    c2 = np.concatenate([cos, cos], axis=0)
    s2 = np.concatenate([-sin, sin], axis=0)
    tri = np.triu(np.ones((128, 128), np.float32)).astype(BF)
    onc = np.ones((128, 1), BF)
    oncf = np.ones((128, 1), np.float32)
    onr = np.ones((1, 128), BF)
    onrf = np.ones((1, 128), np.float32)
    onm = np.ones((128, 128), BF)
    return c2, s2, tri, onc, oncf, onr, onrf, onm


def kernel(L0_attn_x, L0_attn_wq, L0_attn_wk, L0_attn_wv, L0_attn_wo,
           L0_ffn_weight, L0_ffn_bias):
    x = np.ascontiguousarray(np.asarray(L0_attn_x, dtype=np.float32))
    wq = np.ascontiguousarray(np.asarray(L0_attn_wq, dtype=np.float32))
    wk = np.ascontiguousarray(np.asarray(L0_attn_wk, dtype=np.float32))
    wv = np.ascontiguousarray(np.asarray(L0_attn_wv, dtype=np.float32))
    wo = np.ascontiguousarray(np.asarray(L0_attn_wo, dtype=np.float32))
    wf = np.ascontiguousarray(np.asarray(L0_ffn_weight, dtype=np.float32))
    fb = np.ascontiguousarray(np.asarray(L0_ffn_bias, dtype=np.float32)).reshape(1, E)
    c2, s2, tri, onc, oncf, onr, onrf, onm = _consts()
    nc = build_nc()
    in_maps = []
    for b in range(8):
        in_maps.append({
            "x": x[b], "wq": wq, "wk": wk, "wv": wv, "wo": wo, "wf": wf,
            "fbias": fb, "c2t": c2, "s2t": s2, "tri": tri, "onc": onc,
            "oncf": oncf, "onr": onr, "onrf": onrf, "onm": onm,
        })
    res = run_bass_kernel_spmd(nc, in_maps, core_ids=list(range(8)))
    out = np.stack([np.asarray(r["y"], dtype=np.float32) for r in res.results], axis=0)
    return out
